# version 10
# baseline (speedup 1.0000x reference)
import contextlib
import math
import numpy as np
import ml_dtypes

import concourse.bass as bass
import concourse.mybir as mybir
from concourse.bass_utils import run_bass_kernel_spmd
from concourse.ap import AP

F32 = mybir.dt.float32
BF16 = mybir.dt.bfloat16
AF = mybir.ActivationFunctionType
ALU = mybir.AluOpType
AX = mybir.AxisListType

B = 4
S = 4096
H = 2048
NH = 16
HD = 128
IN = 5632
TOK = 2048
NT = 16
NTF = 32
NEG = -1.0e38
VSTRIDE = 132

NTK = [2 * lt + 2 for lt in range(NT)]
NCH = [(ntk * 128 + 511) // 512 for ntk in NTK]
NGR = [(ntk + 3) // 4 for ntk in NTK]
CH_BEFORE = [0] * NT
GR_BEFORE = [0] * NT
for _lt in range(1, NT):
    CH_BEFORE[_lt] = CH_BEFORE[_lt - 1] + NCH[_lt - 1]
    GR_BEFORE[_lt] = GR_BEFORE[_lt - 1] + NGR[_lt - 1]
TOT_CH = CH_BEFORE[-1] + NCH[-1]
TOT_GR = GR_BEFORE[-1] + NGR[-1]
BIAS_OFF = [(30 - 2 * lt) * 128 for lt in range(NT)]

TK_BEFORE = [0] * NT
for _lt in range(1, NT):
    TK_BEFORE[_lt] = TK_BEFORE[_lt - 1] + NTK[_lt - 1]
TOT_TK = TK_BEFORE[-1] + NTK[-1]

CUMT = []
_c = 0
for _h in range(NH):
    for _lt in range(NT):
        _c += NTK[_lt]
        CUMT.append(_c)


def parity_cum(n):
    ce = [[0] * (n + 1), [0] * (n + 1)]
    for i in range(n):
        ce[0][i + 1] = ce[0][i] + (1 if i % 2 == 0 else 0)
        ce[1][i + 1] = ce[1][i] + (1 if i % 2 == 1 else 0)
    return ce


CNT_T = parity_cum(NH * TOT_TK)
CNT_TP = parity_cum(NH * TOT_TK // 2)
CNT_TR = parity_cum(8 * NTF)
CNT_TR2 = parity_cum(8 * NT)
CNT_CX = parity_cum(8 * NT)
NEV = [(n + 1) // 2 for n in NCH]
EV_BEFORE = [0] * NT
for _lt in range(1, NT):
    EV_BEFORE[_lt] = EV_BEFORE[_lt - 1] + NEV[_lt - 1]
TOT_EV = EV_BEFORE[-1] + NEV[-1]


def GEV(h, lt):
    return h * TOT_EV + EV_BEFORE[lt]


def GP(h, lt, pr):
    return (h * TOT_TK + TK_BEFORE[lt]) // 2 + pr


def GT(h, lt, kt):
    return h * TOT_TK + TK_BEFORE[lt] + kt


def GC(h, lt, ci):
    return h * TOT_CH + CH_BEFORE[lt] + ci


def p2_sc_toks(L):
    lt = L % NT
    tk = []
    if lt == 0:
        tk.append(('ld', 0))
    tk += [('sc', ci) for ci in range(NCH[lt])]
    return tk


def p2_pv_toks(L):
    npr = NTK[L % NT] // 2
    tk = [('hd', 0), ('pt', 0)]
    for pr in range(1, npr):
        tk += [('pt', pr), ('pv', pr - 1)]
    tk.append(('pv', npr - 1))
    return tk


def p2_master(L):
    a = p2_sc_toks(L) if L < NH * NT else []
    b = p2_pv_toks(L - 2) if L >= 2 else []
    toks = []
    if not a:
        toks = list(b)
    else:
        ratio = max(1, len(b) // len(a)) if b else 0
        si = pi = 0
        while si < len(a) or pi < len(b):
            if si < len(a):
                toks.append(a[si])
                si += 1
            for _ in range(ratio):
                if pi < len(b):
                    toks.append(b[pi])
                    pi += 1
            if si >= len(a):
                while pi < len(b):
                    toks.append(b[pi])
                    pi += 1
    if L < NH * NT:
        toks.append(('atail', 0))
    if L >= 2:
        toks.append(('btail', 0))
        toks.append(('oev', 0))
    if L < NH * NT:
        toks.append(('exp', 0))
    return toks


ARENA_BYTES = 208896
O_XNTF = 0
O_C = 131072
O_SSGN = 147456
O_W1 = 163840
O_CMY = 102400
O_SMY = 110592
O_XNTM = 0
O_P1C = 65536
O_CTX = 0
O_K2 = 65536
O_V2 = 81920
O_Q2 = 98816
O_B2 = 107008
O_S2 = 123392
O_P2 = 172544
O_PT2 = 188928
O_MC = 205312
O_NM = 205408
O_INV = 205456
O_WD3 = 65536
O_CXT = 131072
O_SO3 = 139264
O_ACC = 0
O_XNH = 32768
O_HDN = 49152
O_WG = 60416
O_WU = 68608
O_W4 = 76800
O_DS = 99328
O_XTI = 107520
O_STI = 115712
O_TMP = 123904
O_TMP2 = 132096
O_OUTS = 140288
O_SG4 = 156672
O_SU4 = 160768
O_GS4 = 164864
O_US4 = 165056
O_XNH2 = 165888
O_XTI2 = 182272
O_STI2 = 190464
P0_OH = O_W1
P0_MK = O_W1 + 4096
P0_BT = O_W1 + 8192
P0_BV = O_W1 + 12288
A_XF = O_W1
A_SQ1 = O_W1 + 16384
A_XN2 = O_W1 + 24576
A_ST = O_W1 + 32768
B_WK = O_W1
B_KO = O_W1 + 8192
B_T1 = O_W1 + 11264
B_T2 = O_W1 + 13312
B_WV = O_W1
B_VO = O_W1 + 32768
B_WQ = O_W1
B_WQ2 = 118784
B_QO = O_W1 + 32768
Q_T1 = O_W1 + 35840
Q_T2 = O_W1 + 37888
C_XF = O_P1C
C_SQ = O_P1C + 16384
C_XN = O_P1C + 24576
C_ST = O_P1C + 32768


def build_nc(debug_taps=False, upto=7):
    nc = bass.Bass("TRN2", target_bir_lowering=False,
                   detect_race_conditions=False)

    d_xf = nc.dram_tensor("x_full", [S, H], F32, kind="ExternalInput")
    d_xm = nc.dram_tensor("x_my", [TOK, H], F32, kind="ExternalInput")
    d_qkv = nc.dram_tensor("qkv_w", [H, 3 * H], F32, kind="ExternalInput")
    d_dw = nc.dram_tensor("dense_w", [H, H], F32, kind="ExternalInput")
    d_gT = nc.dram_tensor("gateT", [H, IN], F32, kind="ExternalInput")
    d_uT = nc.dram_tensor("upT", [H, IN], F32, kind="ExternalInput")
    d_dT = nc.dram_tensor("downT", [IN, H], F32, kind="ExternalInput")
    d_rel = nc.dram_tensor("rel", [32, NH], F32, kind="ExternalInput")
    d_c31 = nc.dram_tensor("c31", [NH, 1], F32, kind="ExternalInput")
    d_oh = nc.dram_tensor("onehot", [32, 4223], F32, kind="ExternalInput")
    d_mk = nc.dram_tensor("maskv", [NH, 4223], F32, kind="ExternalInput")
    d_C = nc.dram_tensor("ropeC", [128, S], F32, kind="ExternalInput")
    d_Sg = nc.dram_tensor("ropeS", [128, S], F32, kind="ExternalInput")
    d_Cm = nc.dram_tensor("ropeCm", [128, TOK], F32, kind="ExternalInput")
    d_Sm = nc.dram_tensor("ropeSm", [128, TOK], F32, kind="ExternalInput")
    d_id = nc.dram_tensor("ident", [128, 128], BF16, kind="ExternalInput")
    d_gs = nc.dram_tensor("gate_sT", [128, 44], F32, kind="ExternalInput")
    d_us = nc.dram_tensor("up_sT", [128, 44], F32, kind="ExternalInput")
    d_ds = nc.dram_tensor("ds_bc", [128, H], F32, kind="ExternalInput")

    d_out = nc.dram_tensor("out", [TOK, H], F32, kind="ExternalOutput")

    d_qkvb = nc.dram_tensor("qkv_bf", [H, 3 * H], BF16, kind="Internal")
    d_dwb = nc.dram_tensor("dense_bf", [H, H], BF16, kind="Internal")
    d_gTb = nc.dram_tensor("gateT_bf", [H, IN], BF16, kind="Internal")
    d_uTb = nc.dram_tensor("upT_bf", [H, IN], BF16, kind="Internal")
    d_dTb = nc.dram_tensor("downT_bf", [IN, H], BF16, kind="Internal")
    d_brev = nc.dram_tensor("biasrev", [NH, 4223], F32, kind="Internal")
    d_bexp = nc.dram_tensor("biasexp", [NH * 128, S], BF16, kind="Internal")
    d_K = nc.dram_tensor("Kd", [NH, 128, S], BF16, kind="Internal")
    d_V = nc.dram_tensor("Vd", [S, H], BF16, kind="Internal")
    d_Q = nc.dram_tensor("Qd", [NH, 128, TOK], BF16, kind="Internal")
    d_xnm = nc.dram_tensor("xnTmy", [NH, 128, TOK], BF16, kind="Internal")
    d_s = nc.dram_tensor("sd", [TOK, H], F32, kind="Internal")

    taps = {}
    if debug_taps:
        for nm_, sh_, dt_ in [("t_xnm", [NH, 128, TOK], BF16),
                              ("t_K", [NH, 128, S], BF16),
                              ("t_V", [S, H], BF16),
                              ("t_Q", [NH, 128, TOK], BF16),
                              ("t_bexp", [NH * 128, S], BF16),
                              ("t_s", [TOK, H], F32)]:
            taps[nm_] = nc.dram_tensor(nm_, sh_, dt_, kind="ExternalOutput")

    es = contextlib.ExitStack()
    with es:
        arena = es.enter_context(
            nc.sbuf_tensor("arena", [128, ARENA_BYTES // 4], F32))
        ident = es.enter_context(nc.sbuf_tensor("identsb", [128, 128], BF16))
        ones32 = es.enter_context(nc.sbuf_tensor("ones32", [128, 32], BF16))
        rel_sb = es.enter_context(nc.sbuf_tensor("rel_sb", [32, NH], F32))
        c31_sb = es.enter_context(nc.sbuf_tensor("c31_sb", [NH, 1], F32))

        arena_bf = arena.bitcast(BF16)
        PS_F32 = arena.ap().ap[0][0]
        PS_BF = arena_bf.ap().ap[0][0]

        def vf(off_bytes, n):
            assert off_bytes % 4 == 0 and off_bytes + 4 * n <= ARENA_BYTES
            return AP(tensor=arena, offset=off_bytes // 4,
                      ap=[[PS_F32, 128], [1, n]])

        def vfd(off_bytes, dims):
            return AP(tensor=arena, offset=off_bytes // 4,
                      ap=[[PS_F32, 128]] + dims)

        def vb(off_bytes, n):
            assert off_bytes % 2 == 0 and off_bytes + 2 * n <= ARENA_BYTES
            return AP(tensor=arena_bf, offset=off_bytes // 2,
                      ap=[[PS_BF, 128], [1, n]])

        def vbd(off_bytes, dims):
            return AP(tensor=arena_bf, offset=off_bytes // 2,
                      ap=[[PS_BF, 128]] + dims)

        def vf16(off_bytes, n):
            return AP(tensor=arena, offset=off_bytes // 4,
                      ap=[[PS_F32, NH], [1, n]])

        def vf32p(off_bytes, n):
            return AP(tensor=arena, offset=off_bytes // 4,
                      ap=[[PS_F32, 32], [1, n]])

        sem_names = [
            "ph",
            "s0L", "sOH", "sMK", "sBMM", "sBEV", "sBV", "sBEXP",
            "sCK", "sCV", "sCQ", "sCD", "sCG", "sCU", "sC4",
            "sXF0", "sXF1", "sSQ", "sVAR", "sSD", "sNB", "sXN", "sTR", "sTEVD", "sTEVS",
            "sWK", "sKMM", "sKEV", "sKST",
            "sWV", "sVMM", "sVEV", "sVST",
            "sXF2a", "sXF2b", "sSQ2", "sVAR2", "sSD2", "sNB2", "sXN2", "sTR2", "sTEVD2",
            "sTEVS2", "sXMYST",
            "sWQ0", "sWQ1", "sQMM", "sQEV", "sQST",
            "sKL", "sVL", "sQL", "sBL", "sVONE", "sHD", "sSMM", "sSEV", "sMX",
            "sNM", "sEXP", "sPT", "sPTEVD", "sPTEVS", "sPV", "sINV", "sOEV",
            "sOT",
            "sWD", "sDMM", "sDEV", "sDST", "sOTEVD", "sOTEVS",
            "vch", "sP4C", "sXH", "sWG", "sWU", "sW4", "sGU", "sSG", "sSU", "sHD4",
            "sDNMM", "sACC", "sXT", "sST", "sT1", "sT2", "sCMB", "sOUT",
        ]
        SM = {}
        for n in sem_names:
            SM[n] = es.enter_context(nc.semaphore(n))

        PSQ = [es.enter_context(nc.psum_tensor(f"psq{i}", [128, 1024], F32))
               for i in range(4)]

        def psap(i, w, off=0):
            tq = PSQ[i // 2]
            return AP(tensor=tq, offset=512 * (i % 2) + off,
                      ap=[[1024, 128], [1, w]])

        def pspair(i2, dims):
            return AP(tensor=PSQ[i2], offset=0, ap=[[1024, 128]] + dims)

        block = es.enter_context(nc.Block())

        NBARS = 7
        (BAR_A, BAR_B, BAR_C, BAR_D, BAR_E, BAR_F, BAR_G) = range(1, 8)

        def bar(eng, k):
            eng.nop().then_inc(SM["ph"], 1)
            eng.wait_ge(SM["ph"], 5 * k)

        @block.sync
        def _(sync):
            sync.dma_start(out=ident[:], in_=d_id.ap()).then_inc(SM["s0L"], 16)
            sync.dma_start(out=rel_sb[:], in_=d_rel.ap()).then_inc(SM["s0L"], 16)
            sync.dma_start(out=c31_sb[:], in_=d_c31.ap()).then_inc(SM["s0L"], 16)
            sync.dma_start(out=vf(O_C, S), in_=d_C.ap()).then_inc(SM["s0L"], 16)
            sync.dma_start(out=vf(O_SSGN, S), in_=d_Sg.ap()).then_inc(SM["s0L"], 16)
            for ci in range(9):
                w = min(512, 4223 - ci * 512)
                if ci >= 1:
                    sync.wait_ge(SM["sBMM"], ci)
                    sync.wait_ge(SM["sBEV"], 2 * ci)
                sync.dma_start(out=vf32p(P0_OH + (ci % 2) * 2048, w),
                               in_=d_oh.ap()[:, ci * 512:ci * 512 + w]
                               ).then_inc(SM["sOH"], 16)
                sync.dma_start(out=vf16(P0_MK + (ci % 2) * 2048, w),
                               in_=d_mk.ap()[:, ci * 512:ci * 512 + w]
                               ).then_inc(SM["sMK"], 16)
            bar(sync, BAR_A)
            if upto < 1:
                return

            for i in range(NTF):
                if i >= 2:
                    sync.wait_ge(SM["sXN"], i - 1)
                sync.dma_start(out=vf(A_XF + (i % 2) * 8192, H),
                               in_=d_xf.ap()[i * 128:(i + 1) * 128, :]
                               ).then_inc(SM["sXF0" if i % 2 == 0 else "sXF1"], 16)
            bar(sync, BAR_B)
            if upto < 2:
                return

            sync.wait_ge(SM["sCK"], 16)
            for c in range(NH):
                if c >= 2:
                    sync.wait_ge(SM["sKMM"], 8 * (c - 1))
                if c >= 1:
                    sync.wait_ge(SM["sWK"], 16 * c)
                src = AP(tensor=d_qkvb, offset=H + c * 128,
                         ap=[[3 * H, 128], [128 * 3 * H, NH], [1, 128]])
                sync.dma_start(out=vbd(B_WK + (c % 2) * 4096, [[128, NH], [1, 128]]),
                               in_=src).then_inc(SM["sWK"], 16)
            sync.wait_ge(SM["sCV"], 16)
            sync.wait_ge(SM["sKMM"], 128)
            sync.wait_ge(SM["sKST"], 16 * 128)
            for vc in range(4):
                if vc >= 2:
                    sync.wait_ge(SM["sVMM"], 32 * (vc - 1))
                if vc >= 1:
                    sync.wait_ge(SM["sWV"], 16 * vc)
                src = AP(tensor=d_qkvb, offset=2 * H + vc * 512,
                         ap=[[3 * H, 128], [128 * 3 * H, NH], [1, 512]])
                sync.dma_start(out=vbd(B_WV + (vc % 2) * 16384, [[512, NH], [1, 512]]),
                               in_=src).then_inc(SM["sWV"], 16)
            bar(sync, BAR_C)
            if upto < 3:
                return

            for j in range(NT):
                if j >= 2:
                    sync.wait_ge(SM["sXN2"], j - 1)
                sync.dma_start(out=vf(C_XF + (j % 2) * 8192, H),
                               in_=d_xm.ap()[j * 128:(j + 1) * 128, :]
                               ).then_inc(SM["sXF2a" if j % 2 == 0 else "sXF2b"], 16)
            bar(sync, BAR_D)
            if upto < 4:
                return

            sync.dma_start(out=vf(O_CMY, TOK), in_=d_Cm.ap()).then_inc(SM["s0L"], 16)
            sync.dma_start(out=vf(O_SMY, TOK), in_=d_Sm.ap()).then_inc(SM["s0L"], 16)
            sync.wait_ge(SM["sCQ"], 16)
            for ch in range(2):
                src = AP(tensor=d_qkvb, offset=ch * 1024,
                         ap=[[3 * H, 128], [128 * 3 * H, NH], [1, 1024]])
                sync.dma_start(out=vbd(B_WQ if ch == 0 else B_WQ2,
                                       [[1024, NH], [1, 1024]]),
                               in_=src).then_inc(SM["sWQ0" if ch == 0
                                                 else "sWQ1"], 16)
            for h in range(2):
                if h == 1:
                    sync.wait_ge(SM["sKL"], 16)
                    sync.wait_ge(SM["sVL"], 16)
                sync.dma_start(out=vb(O_K2 + (h % 2) * 8192, S),
                               in_=AP(tensor=d_K, offset=h * 128 * S,
                                      ap=[[S, 128], [1, S]])
                               ).then_inc(SM["sKL"], 16)
                vdst = vbd(O_V2 + (h % 2) * 8448, [[VSTRIDE, 32], [1, 128]])
                vsrc = AP(tensor=d_V, offset=h * 128,
                          ap=[[H, 128], [128 * H, 32], [1, 128]])
                sync.dma_start(out=vdst, in_=vsrc).then_inc(SM["sVL"], 16)
            bar(sync, BAR_E)
            if upto < 5:
                return

            sync.wait_ge(SM["sVONE"], 2)
            sync.wait_ge(SM["sBEXP"], 16 * NH)
            for h in range(NH):
                if h >= 2:
                    sync.wait_ge(SM["sPV"], 16 * (h - 1))
                if h >= 1:
                    sync.wait_ge(SM["sQL"], 16 * h)
                    sync.wait_ge(SM["sBL"], 16 * h)
                if h >= 2:
                    sync.wait_ge(SM["sKL"], 16 * h)
                    sync.wait_ge(SM["sVL"], 16 * h)
                    sync.dma_start(out=vb(O_K2 + (h % 2) * 8192, S),
                                   in_=AP(tensor=d_K, offset=h * 128 * S,
                                          ap=[[S, 128], [1, S]])
                                   ).then_inc(SM["sKL"], 16)
                    vdst = vbd(O_V2 + (h % 2) * 8448,
                               [[VSTRIDE, 32], [1, 128]])
                    vsrc = AP(tensor=d_V, offset=h * 128,
                              ap=[[H, 128], [128 * H, 32], [1, 128]])
                    sync.dma_start(out=vdst, in_=vsrc).then_inc(SM["sVL"], 16)
                sync.dma_start(out=vb(O_Q2 + (h % 2) * 4096, TOK),
                               in_=AP(tensor=d_Q, offset=h * 128 * TOK,
                                      ap=[[TOK, 128], [1, TOK]])
                               ).then_inc(SM["sQL"], 16)
                sync.dma_start(out=vb(O_B2 + (h % 2) * 8192, S),
                               in_=AP(tensor=d_bexp, offset=h * 128 * S,
                                      ap=[[S, 128], [1, S]])
                               ).then_inc(SM["sBL"], 16)
            bar(sync, BAR_F)
            if upto < 6:
                return

            sync.wait_ge(SM["sCD"], 16)
            for hc in range(4):
                if hc >= 1:
                    sync.wait_ge(SM["sWD"], 16 * hc)
                src = AP(tensor=d_dwb, offset=hc * 512,
                         ap=[[H, 128], [128 * H, NH], [1, 512]])
                dstw = AP(tensor=arena_bf, offset=O_WD3 // 2 + hc * 512,
                          ap=[[PS_BF, 128], [2048, NH], [1, 512]])
                sync.dma_start(out=dstw, in_=src).then_inc(SM["sWD"], 16)
            bar(sync, BAR_G)
            if upto < 7:
                return

            sync.dma_start(out=vf(O_GS4, 44), in_=d_gs.ap()).then_inc(SM["sP4C"], 16)
            sync.dma_start(out=vf(O_US4, 44), in_=d_us.ap()).then_inc(SM["sP4C"], 16)
            sync.dma_start(out=vf(O_DS, H), in_=d_ds.ap()).then_inc(SM["sP4C"], 16)
            sync.wait_ge(SM["sCG"], 16)
            sync.wait_ge(SM["sCU"], 16)
            sync.wait_ge(SM["sC4"], 16)
            for qr in range(4):
                for G_ in range(4):
                    for icl in range(11):
                        ic = 11 * G_ + icl
                        gi = 44 * qr + ic
                        if gi >= 2:
                            sync.wait_ge(SM["sGU"], gi - 1)
                        if gi >= 1:
                            sync.wait_ge(SM["sWG"], 16 * gi)
                            sync.wait_ge(SM["sWU"], 16 * gi)
                        src = AP(tensor=d_gTb, offset=ic * 128,
                                 ap=[[IN, 128], [128 * IN, NH], [1, 128]])
                        sync.dma_start(out=vbd(O_WG + (gi % 2) * 4096,
                                               [[128, NH], [1, 128]]),
                                       in_=src).then_inc(SM["sWG"], 16)
                        src = AP(tensor=d_uTb, offset=ic * 128,
                                 ap=[[IN, 128], [128 * IN, NH], [1, 128]])
                        sync.dma_start(out=vbd(O_WU + (gi % 2) * 4096,
                                               [[128, NH], [1, 128]]),
                                       in_=src).then_inc(SM["sWU"], 16)

        @block.gpsimd
        def _(g):
            g.dma_start(out=d_qkvb.ap()[:, H:2 * H],
                        in_=d_qkv.ap()[:, H:2 * H]).then_inc(SM["sCK"], 16)
            for ci in range(9):
                w = min(512, 4223 - ci * 512)
                g.wait_ge(SM["sBEV"], 2 * (ci + 1))
                if ci >= 1:
                    g.wait_ge(SM["sBV"], 16 * ci)
                g.dma_start(out=AP(tensor=d_brev, offset=ci * 512,
                                   ap=[[4223, NH], [1, w]]),
                            in_=vf16(P0_BV + (ci % 2) * 2048, w)
                            ).then_inc(SM["sBV"], 16)
            g.wait_ge(SM["sBV"], 16 * 9)
            for hh in range(NH):
                toe_src = AP(tensor=d_brev, offset=hh * 4223 + 127,
                             ap=[[128, 32], [-1, 128], [1, 128]])
                toe_dst = AP(tensor=d_bexp, offset=hh * 128 * S,
                             ap=[[128, 32], [S, 128], [1, 128]])
                g.dma_start(out=toe_dst, in_=toe_src).then_inc(SM["sBEXP"], 16)
            g.dma_start(out=d_qkvb.ap()[:, 2 * H:3 * H],
                        in_=d_qkv.ap()[:, 2 * H:3 * H]).then_inc(SM["sCV"], 16)
            g.dma_start(out=d_qkvb.ap()[:, 0:H],
                        in_=d_qkv.ap()[:, 0:H]).then_inc(SM["sCQ"], 16)
            bar(g, BAR_A)
            if upto < 1:
                return
            bar(g, BAR_B)
            if upto < 2:
                return

            for c in range(NH):
                for tc in range(8):
                    kk = 8 * c + tc
                    g.wait_ge(SM["sKEV"], kk + 1)
                    if kk >= 1:
                        g.wait_ge(SM["sKST"], 16 * kk)
                    dst = AP(tensor=d_K, offset=c * 128 * S + tc * 512,
                             ap=[[S, 128], [1, 512]])
                    g.dma_start(out=dst, in_=vb(B_KO + (kk % 3) * 1024, 512)
                                ).then_inc(SM["sKST"], 16)
            for vc in range(4):
                for tt in range(NTF):
                    vv = 32 * vc + tt
                    g.wait_ge(SM["sVEV"], vv + 1)
                    if vv >= 1:
                        g.wait_ge(SM["sVST"], 16 * vv)
                    dst = AP(tensor=d_V, offset=tt * 128 * H + vc * 512,
                             ap=[[H, 128], [1, 512]])
                    g.dma_start(out=dst, in_=vb(B_VO + (vv % 3) * 1024, 512)
                                ).then_inc(SM["sVST"], 16)
            g.wait_ge(SM["sKST"], 16 * 128)
            g.wait_ge(SM["sVST"], 16 * 128)
            bar(g, BAR_C)
            if upto < 3:
                return
            bar(g, BAR_D)
            if upto < 4:
                return

            src = vbd(O_XNTM, [[TOK, NH], [1, TOK]])
            g.dma_start(out=AP(tensor=d_xnm, offset=0,
                               ap=[[TOK, 128], [128 * TOK, NH], [1, TOK]]),
                        in_=src).then_inc(SM["sXMYST"], 16)
            for tc in range(4):
                for c in range(NH):
                    qq = 16 * tc + c
                    g.wait_ge(SM["sQEV"], qq + 1)
                    if qq >= 1:
                        g.wait_ge(SM["sQST"], 16 * qq)
                    dst = AP(tensor=d_Q, offset=c * 128 * TOK + tc * 512,
                             ap=[[TOK, 128], [1, 512]])
                    g.dma_start(out=dst, in_=vb(B_QO + (qq % 3) * 1024, 512)
                                ).then_inc(SM["sQST"], 16)
            g.wait_ge(SM["sQST"], 16 * 64)
            g.wait_ge(SM["sXMYST"], 16)
            if debug_taps:
                g.dma_start(out=taps["t_xnm"].ap(), in_=d_xnm.ap()).then_inc(SM["sOUT"], 16)
                g.dma_start(out=taps["t_K"].ap(), in_=d_K.ap()).then_inc(SM["sOUT"], 16)
                g.dma_start(out=taps["t_V"].ap(), in_=d_V.ap()).then_inc(SM["sOUT"], 16)
                g.dma_start(out=taps["t_Q"].ap(), in_=d_Q.ap()).then_inc(SM["sOUT"], 16)
                g.dma_start(out=taps["t_bexp"].ap(), in_=d_bexp.ap()).then_inc(SM["sOUT"], 16)
                g.wait_ge(SM["sOUT"], 16 * 5)
            bar(g, BAR_E)
            if upto < 5:
                return

            g.dma_start(out=d_dwb.ap(), in_=d_dw.ap()).then_inc(SM["sCD"], 16)
            g.dma_start(out=d_gTb.ap(), in_=d_gT.ap()).then_inc(SM["sCG"], 16)
            g.dma_start(out=d_uTb.ap(), in_=d_uT.ap()).then_inc(SM["sCU"], 16)
            g.dma_start(out=d_dTb.ap(), in_=d_dT.ap()).then_inc(SM["sC4"], 16)
            bar(g, BAR_F)
            if upto < 6:
                return

            for lt in range(NT):
                for hc in range(4):
                    dd = 4 * lt + hc
                    g.wait_ge(SM["sDEV"], dd + 1)
                    if dd >= 1:
                        g.wait_ge(SM["sDST"], 16 * dd)
                    dst = AP(tensor=d_s, offset=lt * 128 * H + hc * 512,
                             ap=[[H, 128], [1, 512]])
                    g.dma_start(out=dst, in_=vf(O_SO3 + (dd % 3) * 2048, 512)
                                ).then_inc(SM["sDST"], 16)
            g.wait_ge(SM["sDST"], 16 * 64)
            ntap = 6 if debug_taps else 0
            if debug_taps:
                g.dma_start(out=taps["t_s"].ap(), in_=d_s.ap()).then_inc(SM["sOUT"], 16)
                g.wait_ge(SM["sOUT"], 16 * 6)
            bar(g, BAR_G)
            if upto < 7:
                return

            def xnh_load(qr):
                src = AP(tensor=d_xnm, offset=qr * 512,
                         ap=[[TOK, 128], [128 * TOK, NH], [1, 512]])
                g.dma_start(out=vbd(O_XNH if qr % 2 == 0 else O_XNH2,
                                    [[512, NH], [1, 512]]),
                            in_=src).then_inc(SM["sXH"], 16)

            def xs_load(cb):
                row0 = cb * 128
                g.dma_start(out=vf(O_XTI if cb % 2 == 0 else O_XTI2, H),
                            in_=d_xm.ap()[row0:row0 + 128, :]
                            ).then_inc(SM["sXT"], 16)
                g.dma_start(out=vf(O_STI if cb % 2 == 0 else O_STI2, H),
                            in_=d_s.ap()[row0:row0 + 128, :]
                            ).then_inc(SM["sST"], 16)

            xnh_load(0)
            g.wait_ge(SM["sXH"], 16)
            xnh_load(1)
            xs_load(0)
            for cb in range(16):
                qr = cb // 4
                if cb % 4 == 0 and qr + 2 < 4:
                    g.wait_ge(SM["sXH"], 16 * (qr + 2))
                    g.wait_ge(SM["sGU"], 44 * (qr + 1))
                    xnh_load(qr + 2)
                if cb + 1 < 16:
                    g.wait_ge(SM["sXT"], 16 * (cb + 1))
                    g.wait_ge(SM["sST"], 16 * (cb + 1))
                    if cb >= 1:
                        g.wait_ge(SM["sCMB"], cb)
                    xs_load(cb + 1)
                g.wait_ge(SM["sCMB"], cb + 1)
                if cb >= 1:
                    g.wait_ge(SM["sOUT"], 16 * (cb + ntap))
                g.dma_start(out=d_out.ap()[cb * 128:(cb + 1) * 128, :],
                            in_=vf(O_OUTS + (cb % 2) * 8192, H)
                            ).then_inc(SM["sOUT"], 16)
            g.wait_ge(SM["sOUT"], 16 * (16 + ntap))

        @block.tensor
        def _(t):
            t.wait_ge(SM["s0L"], 16 * 5)
            for ci in range(9):
                w = min(512, 4223 - ci * 512)
                t.wait_ge(SM["sOH"], 16 * (ci + 1))
                if ci >= 2:
                    t.wait_ge(SM["sBEV"], 2 * (ci - 1))
                t.matmul(AP(tensor=PSQ[0], offset=512 * (ci % 2), ap=[[1024, NH], [1, w]]), rel_sb[:],
                         vf32p(P0_OH + (ci % 2) * 2048, w),
                         start=True, stop=True).then_inc(SM["sBMM"], 1)
            t.wait_ge(SM["sBMM"], 9)
            bar(t, BAR_A)
            if upto < 1:
                return

            def transposes(n_tiles, xn_off, sTRn, sXNn, sTEVDn, sTEVSn, cnt):
                for i in range(n_tiles):
                    t.wait_ge(SM[sXNn], i + 1)
                    xn = vb(xn_off + (i % 2) * 4096, H)
                    for k in range(8):
                        q = 8 * i + k
                        e = q % 2
                        sem = SM[sTEVDn] if e == 0 else SM[sTEVSn]
                        if q >= 2:
                            t.wait_ge(sem, cnt[e][q])
                        tq = PSQ[1 + q % 2]
                        for half in range(2):
                            hc = 2 * k + half
                            lhs = AP(tensor=arena_bf,
                                     offset=xn.offset + hc * 128,
                                     ap=[[PS_BF, 128], [1, 128]])
                            dst = AP(tensor=tq, offset=512 * half,
                                     ap=[[1024, 128], [1, 128]])
                            ins = t.matmul(dst, lhs, ident[:],
                                           start=True, stop=True)
                        ins.then_inc(SM[sTRn], 1)

            transposes(NTF, A_XN2, "sTR", "sXN", "sTEVD", "sTEVS", CNT_TR)
            t.wait_ge(SM["sTR"], 8 * NTF)
            bar(t, BAR_B)
            if upto < 2:
                return

            for c in range(NH):
                t.wait_ge(SM["sWK"], 16 * (c + 1))
                for tc in range(8):
                    kk = 8 * c + tc
                    if kk >= 2:
                        t.wait_ge(SM["sKEV"], kk - 1)
                    for hc in range(NH):
                        lhs = AP(tensor=arena_bf,
                                 offset=(B_WK + (c % 2) * 4096) // 2 + hc * 128,
                                 ap=[[PS_BF, 128], [1, 128]])
                        rhs = AP(tensor=arena_bf,
                                 offset=O_XNTF // 2 + hc * S + tc * 512,
                                 ap=[[PS_BF, 128], [1, 512]])
                        ins = t.matmul(psap(kk % 2, 512), lhs, rhs,
                                       start=(hc == 0), stop=(hc == NH - 1))
                    ins.then_inc(SM["sKMM"], 1)
            for vc in range(4):
                t.wait_ge(SM["sWV"], 16 * (vc + 1))
                for tt in range(NTF):
                    vv = 32 * vc + tt
                    if vv >= 2:
                        t.wait_ge(SM["sVEV"], vv - 1)
                    else:
                        t.wait_ge(SM["sKEV"], 127 + vv)
                    for hc in range(NH):
                        lhs = AP(tensor=arena_bf,
                                 offset=O_XNTF // 2 + hc * S + tt * 128,
                                 ap=[[PS_BF, 128], [1, 128]])
                        rhs = AP(tensor=arena_bf,
                                 offset=(B_WV + (vc % 2) * 16384) // 2 + hc * 512,
                                 ap=[[PS_BF, 128], [1, 512]])
                        ins = t.matmul(psap(vv % 2, 512), lhs, rhs,
                                       start=(hc == 0), stop=(hc == NH - 1))
                    ins.then_inc(SM["sVMM"], 1)
            t.wait_ge(SM["sVMM"], 128)
            bar(t, BAR_C)
            if upto < 3:
                return

            transposes(NT, C_XN, "sTR2", "sXN2", "sTEVD2", "sTEVS2", CNT_TR2)
            t.wait_ge(SM["sTR2"], 8 * NT)
            bar(t, BAR_D)
            if upto < 4:
                return

            for idx in range(8):
                tc = idx // 2
                t.wait_ge(SM["sWQ0" if idx % 2 == 0 else "sWQ1"], 16)
                wqo = B_WQ if idx % 2 == 0 else B_WQ2
                for lc in range(8):
                    qq = 8 * idx + lc
                    if qq >= 2:
                        t.wait_ge(SM["sQEV"], qq - 1)
                    for hc in range(NH):
                        lhs = AP(tensor=arena_bf,
                                 offset=wqo // 2 + hc * 1024 + lc * 128,
                                 ap=[[PS_BF, 128], [1, 128]])
                        rhs = AP(tensor=arena_bf,
                                 offset=O_XNTM // 2 + hc * TOK + tc * 512,
                                 ap=[[PS_BF, 128], [1, 512]])
                        ins = t.matmul(psap(qq % 2, 512), lhs, rhs,
                                       start=(hc == 0), stop=(hc == NH - 1))
                    ins.then_inc(SM["sQMM"], 1)
            t.wait_ge(SM["sQMM"], 64)
            bar(t, BAR_E)
            if upto < 5:
                return

            def emit_scores(L):
                h, lt = L // NT, L % NT
                if lt == 0:
                    t.wait_ge(SM["sKL"], 16 * (h + 1))
                    t.wait_ge(SM["sQL"], 16 * (h + 1))
                    t.wait_ge(SM["sBL"], 16 * (h + 1))
                nk = NTK[lt] * 128
                for ci in range(NCH[lt]):
                    w = min(512, nk - ci * 512)
                    if ci < 2:
                        if L >= 1:
                            t.wait_ge(SM["sSEV"], GEV(L // NT, L % NT))
                    else:
                        t.wait_ge(SM["sSEV"], GEV(h, lt) + ci // 2)
                    lhs = AP(tensor=arena_bf,
                             offset=(O_Q2 + (h % 2) * 4096) // 2 + lt * 128,
                             ap=[[PS_BF, 128], [1, 128]])
                    rhs = AP(tensor=arena_bf,
                             offset=(O_K2 + (h % 2) * 8192) // 2 + ci * 512,
                             ap=[[PS_BF, 128], [1, w]])
                    has_bias = ci >= NCH[lt] - 2
                    ins = t.matmul(psap(ci % 2, w), lhs, rhs,
                                   start=True, stop=not has_bias)
                    if has_bias:
                        brhs = AP(tensor=arena_bf,
                                  offset=(O_B2 + (h % 2) * 8192) // 2
                                  + BIAS_OFF[lt] + ci * 512,
                                  ap=[[PS_BF, 128], [1, w]])
                        ins = t.matmul(psap(ci % 2, w), ident[:], brhs,
                                       start=False, stop=True)
                    ins.then_inc(SM["sSMM"], 1)

            def emit_ptpv(L):
                h, lt = L // NT, L % NT
                ntk = NTK[lt]

                def pt_pair(pr):
                    gp = GP(h, lt, pr)
                    e = gp % 2
                    sem = SM["sPTEVD"] if e == 0 else SM["sPTEVS"]
                    if gp >= 2:
                        t.wait_ge(sem, CNT_TP[e][gp])
                    tq = PSQ[1 + gp % 2]
                    for half in range(2):
                        kt = 2 * pr + half
                        lhs = AP(tensor=arena_bf,
                                 offset=(O_P2 + (L % 2) * 8192) // 2 + kt * 128,
                                 ap=[[PS_BF, 128], [1, 128]])
                        dst = AP(tensor=tq, offset=512 * half,
                                 ap=[[1024, 128], [1, 128]])
                        ins = t.matmul(dst, lhs, ident[:], start=True, stop=True)
                    ins.then_inc(SM["sPT"], 1)

                def pv_pair(pr):
                    gp = GP(h, lt, pr)
                    e = gp % 2
                    sem = SM["sPTEVD"] if e == 0 else SM["sPTEVS"]
                    t.wait_ge(sem, CNT_TP[e][gp + 1])
                    ins = None
                    for half in range(2):
                        kt = 2 * pr + half
                        lhs = AP(tensor=arena_bf,
                                 offset=(O_PT2 + (L % 2) * 8192) // 2 + kt * 128,
                                 ap=[[PS_BF, 128], [1, 128]])
                        rhs = AP(tensor=arena_bf,
                                 offset=(O_V2 + (h % 2) * 8448) // 2 + kt * VSTRIDE,
                                 ap=[[PS_BF, 128], [1, 129]])
                        odst = AP(tensor=PSQ[3], offset=512 * (lt % 2),
                                  ap=[[1024, 128], [1, 129]])
                        ins = t.matmul(odst, lhs, rhs,
                                       start=(kt == 0), stop=(kt == ntk - 1))
                    return ins

                t.wait_ge(SM["sEXP"], L + 1)
                if lt == 0:
                    t.wait_ge(SM["sVL"], 16 * (h + 1))
                if L >= 2:
                    t.wait_ge(SM["sOEV"], L - 1)
                npr = ntk // 2
                pt_pair(0)
                for pr in range(1, npr):
                    pt_pair(pr)
                    pv_pair(pr - 1)
                last = pv_pair(npr - 1)
                last.then_inc(SM["sPV"], 1)

            def scores_thunks(L):
                h, lt = L // NT, L % NT
                nk = NTK[lt] * 128
                th = []
                if lt == 0:
                    def t0():
                        t.wait_ge(SM["sKL"], 16 * (h + 1))
                        t.wait_ge(SM["sQL"], 16 * (h + 1))
                        t.wait_ge(SM["sBL"], 16 * (h + 1))
                    th.append(t0)
                for ci in range(NCH[lt]):
                    def tc(ci=ci):
                        w = min(512, nk - ci * 512)
                        if ci < 2:
                            if L >= 1:
                                t.wait_ge(SM["sSEV"], GEV(h, lt))
                        else:
                            t.wait_ge(SM["sSEV"], GEV(h, lt) + ci // 2)
                        lhs = AP(tensor=arena_bf,
                                 offset=(O_Q2 + (h % 2) * 4096) // 2 + lt * 128,
                                 ap=[[PS_BF, 128], [1, 128]])
                        rhs = AP(tensor=arena_bf,
                                 offset=(O_K2 + (h % 2) * 8192) // 2 + ci * 512,
                                 ap=[[PS_BF, 128], [1, w]])
                        has_bias = ci >= NCH[lt] - 2
                        ins = t.matmul(psap(ci % 2, w), lhs, rhs,
                                       start=True, stop=not has_bias)
                        if has_bias:
                            brhs = AP(tensor=arena_bf,
                                      offset=(O_B2 + (h % 2) * 8192) // 2
                                      + BIAS_OFF[lt] + ci * 512,
                                      ap=[[PS_BF, 128], [1, w]])
                            ins = t.matmul(psap(ci % 2, w), ident[:], brhs,
                                           start=False, stop=True)
                        ins.then_inc(SM["sSMM"], 1)
                    th.append(tc)
                return th

            def ptpv_thunks(L):
                h, lt = L // NT, L % NT
                ntk = NTK[lt]
                npr = ntk // 2
                th = []

                def mk_pt(pr):
                    def f():
                        gp = GP(h, lt, pr)
                        e = gp % 2
                        sem = SM["sPTEVD"] if e == 0 else SM["sPTEVS"]
                        if gp >= 2:
                            t.wait_ge(sem, CNT_TP[e][gp])
                        tq = PSQ[1 + gp % 2]
                        for half in range(2):
                            kt = 2 * pr + half
                            lhs = AP(tensor=arena_bf,
                                     offset=(O_P2 + (L % 2) * 8192) // 2 + kt * 128,
                                     ap=[[PS_BF, 128], [1, 128]])
                            dst = AP(tensor=tq, offset=512 * half,
                                     ap=[[1024, 128], [1, 128]])
                            ins = t.matmul(dst, lhs, ident[:],
                                           start=True, stop=True)
                        ins.then_inc(SM["sPT"], 1)
                    return f

                def mk_pv(pr):
                    def f():
                        gp = GP(h, lt, pr)
                        e = gp % 2
                        sem = SM["sPTEVD"] if e == 0 else SM["sPTEVS"]
                        t.wait_ge(sem, CNT_TP[e][gp + 1])
                        ins = None
                        for half in range(2):
                            kt = 2 * pr + half
                            lhs = AP(tensor=arena_bf,
                                     offset=(O_PT2 + (L % 2) * 8192) // 2 + kt * 128,
                                     ap=[[PS_BF, 128], [1, 128]])
                            rhs = AP(tensor=arena_bf,
                                     offset=(O_V2 + (h % 2) * 8448) // 2 + kt * VSTRIDE,
                                     ap=[[PS_BF, 128], [1, 129]])
                            odst = AP(tensor=PSQ[3], offset=512 * (lt % 2),
                                      ap=[[1024, 128], [1, 129]])
                            ins = t.matmul(odst, lhs, rhs,
                                           start=(kt == 0), stop=(kt == ntk - 1))
                        if pr == npr - 1:
                            ins.then_inc(SM["sPV"], 1)
                    return f

                def head():
                    t.wait_ge(SM["sEXP"], L + 1)
                    if lt == 0:
                        t.wait_ge(SM["sVL"], 16 * (h + 1))
                    if L >= 2:
                        t.wait_ge(SM["sOEV"], L - 1)
                th.append(head)
                th.append(mk_pt(0))
                for pr in range(1, npr):
                    th.append(mk_pt(pr))
                    th.append(mk_pv(pr - 1))
                th.append(mk_pv(npr - 1))
                return th

            for L in range(NH * NT + 2):
                sc_list = scores_thunks(L) if L < NH * NT else []
                pv_list = ptpv_thunks(L - 2) if L >= 2 else []
                si = pi = 0
                for tok in p2_master(L):
                    if tok[0] in ('ld', 'sc'):
                        sc_list[si]()
                        si += 1
                    elif tok[0] in ('hd', 'pt', 'pv'):
                        pv_list[pi]()
                        pi += 1
            t.wait_ge(SM["sPV"], NH * NT)
            bar(t, BAR_F)
            if upto < 6:
                return

            t.wait_ge(SM["sWD"], 16 * 4)
            for lt in range(NT):
                for k in range(8):
                    q = 8 * lt + k
                    e = q % 2
                    sem = SM["sOTEVD"] if e == 0 else SM["sOTEVS"]
                    if q >= 2:
                        t.wait_ge(sem, CNT_CX[e][q])
                    if k == 0 and lt >= 2:
                        t.wait_ge(SM["sDMM"], 4 * (lt - 1))
                    tq = PSQ[1 + q % 2]
                    for half in range(2):
                        c2 = 2 * k + half
                        lhs = AP(tensor=arena_bf,
                                 offset=O_CTX // 2 + lt * 2048 + c2 * 128,
                                 ap=[[PS_BF, 128], [1, 128]])
                        dstp = AP(tensor=tq, offset=512 * half,
                                  ap=[[1024, 128], [1, 128]])
                        ins = t.matmul(dstp, lhs, ident[:], start=True, stop=True)
                    ins.then_inc(SM["sOT"], 1)
                for hc in range(4):
                    dd = 4 * lt + hc
                    t.wait_ge(SM["sOTEVD"], CNT_CX[0][8 * lt + 8])
                    t.wait_ge(SM["sOTEVS"], CNT_CX[1][8 * lt + 8])
                    if dd >= 2:
                        t.wait_ge(SM["sDEV"], dd - 1)
                    for h2 in range(NH):
                        lhs = AP(tensor=arena_bf,
                                 offset=(O_CXT + (lt % 2) * 4096) // 2 + h2 * 128,
                                 ap=[[PS_BF, 128], [1, 128]])
                        rhs = AP(tensor=arena_bf,
                                 offset=O_WD3 // 2 + h2 * 2048 + hc * 512,
                                 ap=[[PS_BF, 128], [1, 512]])
                        ins = t.matmul(psap(dd % 2, 512), lhs, rhs,
                                       start=(h2 == 0), stop=(h2 == NH - 1))
                    ins.then_inc(SM["sDMM"], 1)
            t.wait_ge(SM["sDMM"], 64)
            bar(t, BAR_G)
            if upto < 7:
                return

            for qr in range(4):
                t.wait_ge(SM["sXH"], 16 * (qr + 1))
                for G_ in range(4):
                    for icl in range(11):
                        ic = 11 * G_ + icl
                        gi = 44 * qr + ic
                        t.wait_ge(SM["sWG"], 16 * (gi + 1))
                        t.wait_ge(SM["sWU"], 16 * (gi + 1))
                        if gi >= 2:
                            t.wait_ge(SM["sSG"], gi - 1)
                            t.wait_ge(SM["sSU"], gi - 1)
                        xnho = O_XNH if qr % 2 == 0 else O_XNH2
                        for hc in range(NH):
                            lhs = AP(tensor=arena_bf,
                                     offset=(O_WG + (gi % 2) * 4096) // 2 + hc * 128,
                                     ap=[[PS_BF, 128], [1, 128]])
                            rhs = AP(tensor=arena_bf,
                                     offset=xnho // 2 + hc * 512,
                                     ap=[[PS_BF, 128], [1, 512]])
                            t.matmul(psap(gi % 2, 512), lhs, rhs,
                                     start=(hc == 0), stop=(hc == NH - 1))
                        for hc in range(NH):
                            lhs = AP(tensor=arena_bf,
                                     offset=(O_WU + (gi % 2) * 4096) // 2 + hc * 128,
                                     ap=[[PS_BF, 128], [1, 128]])
                            rhs = AP(tensor=arena_bf,
                                     offset=xnho // 2 + hc * 512,
                                     ap=[[PS_BF, 128], [1, 512]])
                            ins = t.matmul(psap(2 + gi % 2, 512), lhs, rhs,
                                           start=(hc == 0), stop=(hc == NH - 1))
                        ins.then_inc(SM["sGU"], 1)
                    t.wait_ge(SM["sHD4"], 11 * (4 * qr + G_ + 1))
                    for hc in range(4):
                        W_ = (4 * qr + G_) * 4 + hc
                        t.wait_ge(SM["sW4"], 16 * (W_ + 1))
                        for qt in range(4):
                            dn = W_ * 4 + qt
                            if dn >= 2:
                                t.wait_ge(SM["sACC"], dn - 1)
                            for icl in range(11):
                                lhs = AP(tensor=arena_bf,
                                         offset=O_HDN // 2 + icl * 512 + qt * 128,
                                         ap=[[PS_BF, 128], [1, 128]])
                                rhs = AP(tensor=arena_bf,
                                         offset=(O_W4 + (W_ % 2) * 11264) // 2
                                         + icl * 512,
                                         ap=[[PS_BF, 128], [1, 512]])
                                ins = t.matmul(psap(4 + dn % 2, 512), lhs, rhs,
                                               start=(icl == 0), stop=(icl == 10))
                            ins.then_inc(SM["sDNMM"], 1)

        @block.vector
        def _(v):
            VCH = [0]
            v.wait_ge(SM["s0L"], 16 * 5)
            for ci in range(9):
                w = min(512, 4223 - ci * 512)
                v.wait_ge(SM["sBMM"], ci + 1)
                v.wait_ge(SM["sMK"], 16 * (ci + 1))
                if ci >= 2:
                    v.wait_ge(SM["sBV"], 16 * (ci - 1))
                bt = vf16(P0_BT + (ci % 2) * 2048, w)
                v.tensor_scalar(out=bt, in0=AP(tensor=PSQ[0], offset=512 * (ci % 2), ap=[[1024, NH], [1, w]]),
                                scalar1=c31_sb[:, 0:1], scalar2=None,
                                op0=ALU.subtract)
                v.tensor_tensor(out=vf16(P0_BV + (ci % 2) * 2048, w), in0=bt,
                                in1=vf16(P0_MK + (ci % 2) * 2048, w),
                                op=ALU.add).then_inc(SM["sBEV"], 2)
            v.wait_ge(SM["sBEV"], 18)
            bar(v, BAR_A)
            if upto < 1:
                return

            def ln_dve(n_tiles, xf_off, sq_off, st_off, names):
                sXFn, sSQn, sVARn, sSDn, sNBn, sXNn, sTRn, sTEVDn, xnt_off, seqlen = names
                stats = vf(st_off, 16)
                for i in range(n_tiles):
                    sl = (i % 2) * 8
                    xf = vf(xf_off + (i % 2) * 8192, H)
                    sq = vf(sq_off, H)
                    if i >= 2:
                        v.wait_ge(SM[sXNn], i - 1)
                    v.wait_ge(SM[sXFn[i % 2]], 16 * (i // 2 + 1))
                    v.reduce_sum(out=stats[:, sl:sl + 1], in_=xf, axis=AX.X
                                 ).then_inc(SM[sVARn], 1)
                    v.wait_ge(SM[sSQn], i + 1)
                    v.reduce_sum(out=stats[:, sl + 1:sl + 2], in_=sq, axis=AX.X
                                 ).then_inc(SM[sVARn], 1)
                    v.wait_ge(SM[sVARn], 3 * i + 2)
                    vc = VCH[0]
                    v.tensor_scalar_mul(out=stats[:, sl + 2:sl + 3],
                                        in0=stats[:, sl:sl + 1], scalar1=1.0 / H
                                        ).then_inc(SM["vch"], 1)
                    v.tensor_scalar_mul(out=stats[:, sl + 3:sl + 4],
                                        in0=stats[:, sl + 1:sl + 2], scalar1=1.0 / H
                                        ).then_inc(SM["vch"], 1)
                    v.wait_ge(SM["vch"], vc + 2)
                    v.tensor_mul(out=stats[:, sl + 4:sl + 5],
                                 in0=stats[:, sl + 2:sl + 3],
                                 in1=stats[:, sl + 2:sl + 3]).then_inc(SM["vch"], 1)
                    v.wait_ge(SM["vch"], vc + 3)
                    v.tensor_sub(out=stats[:, sl + 5:sl + 6],
                                 in0=stats[:, sl + 3:sl + 4],
                                 in1=stats[:, sl + 4:sl + 5]).then_inc(SM["vch"], 1)
                    v.wait_ge(SM["vch"], vc + 4)
                    v.tensor_scalar_add(out=stats[:, sl + 5:sl + 6],
                                        in0=stats[:, sl + 5:sl + 6],
                                        scalar1=1e-5).then_inc(SM[sVARn], 1)
                    v.wait_ge(SM[sSDn], i + 1)
                    v.reciprocal(out=stats[:, sl + 6:sl + 7],
                                 in_=stats[:, sl + 7:sl + 8]).then_inc(SM["vch"], 1)
                    v.wait_ge(SM["vch"], vc + 5)
                    v.tensor_scalar_mul(out=stats[:, sl + 7:sl + 8],
                                        in0=stats[:, sl + 6:sl + 7], scalar1=-1.0
                                        ).then_inc(SM["vch"], 1)
                    v.wait_ge(SM["vch"], vc + 6)
                    v.tensor_mul(out=stats[:, sl + 7:sl + 8],
                                 in0=stats[:, sl + 7:sl + 8],
                                 in1=stats[:, sl + 2:sl + 3]).then_inc(SM[sNBn], 1)
                    VCH[0] += 6
                    for k in range(8):
                        q = 8 * i + k
                        if q % 2 != 0:
                            continue
                        v.wait_ge(SM[sTRn], q + 1)
                        srcp = AP(tensor=PSQ[1 + q % 2], offset=0,
                                  ap=[[1024, 128], [512, 2], [1, 128]])
                        dst = AP(tensor=arena_bf,
                                 offset=xnt_off // 2 + 2 * k * seqlen + i * 128,
                                 ap=[[PS_BF, 128], [seqlen, 2], [1, 128]])
                        v.tensor_copy(out=dst, in_=srcp).then_inc(SM[sTEVDn], 1)

            ln_dve(NTF, A_XF, A_SQ1, A_ST,
                   (("sXF0", "sXF1"), "sSQ", "sVAR", "sSD", "sNB", "sXN", "sTR",
                    "sTEVD", O_XNTF, S))
            bar(v, BAR_B)
            if upto < 2:
                return

            def rope_dve(pairs, Coff, Soff, ko_off, sMMn, sEVn, sSTn,
                         t1off=B_T1, t2off=B_T2):
                for idx, tc in pairs:
                    v.wait_ge(SM[sMMn], idx + 1)
                    if idx >= 3:
                        v.wait_ge(SM[sSTn], 16 * (idx - 2))
                    ps = psap(idx % 2, 512)
                    t1 = vf(t1off, 512)
                    t2 = vf(t2off, 512)
                    Cs = vf(Coff + tc * 2048, 512)
                    Ss = vf(Soff + tc * 2048, 512)
                    v.tensor_mul(out=t1, in0=ps, in1=Cs)
                    v.tensor_mul(out=t2[0:64, :], in0=ps[64:128, :],
                                 in1=Ss[0:64, :])
                    v.tensor_mul(out=t2[64:128, :], in0=ps[0:64, :],
                                 in1=Ss[64:128, :])
                    v.tensor_add(out=vb(ko_off + (idx % 3) * 1024, 512),
                                 in0=t1, in1=t2).then_inc(SM[sEVn], 1)

            rope_dve([(8 * c + tc, tc) for c in range(NH) for tc in range(8)],
                     O_C, O_SSGN, B_KO, "sKMM", "sKEV", "sKST")
            v.wait_ge(SM["sKEV"], 128)
            bar(v, BAR_C)
            if upto < 3:
                return

            ln_dve(NT, C_XF, C_SQ, C_ST,
                   (("sXF2a", "sXF2b"), "sSQ2", "sVAR2", "sSD2", "sNB2", "sXN2",
                    "sTR2", "sTEVD2", O_XNTM, TOK))
            v.wait_ge(SM["sTEVD2"], CNT_TR2[0][8 * NT])
            v.wait_ge(SM["sNB2"], NT)
            bar(v, BAR_D)
            if upto < 4:
                return

            v.wait_ge(SM["s0L"], 16 * 7)
            rope_dve([(8 * idx + lc, idx // 2) for idx in range(8)
                      for lc in range(8)],
                     O_CMY, O_SMY, B_QO, "sQMM", "sQEV", "sQST",
                     t1off=Q_T1, t2off=Q_T2)
            v.wait_ge(SM["sQEV"], 64)
            bar(v, BAR_E)
            if upto < 5:
                return

            v.memset(ones32[:], 1.0)
            for hb in range(2):
                vdst = AP(tensor=arena_bf,
                          offset=(O_V2 + hb * 8448) // 2 + 128,
                          ap=[[PS_BF, 128], [VSTRIDE, 32]])
                v.tensor_copy(out=vdst, in_=ones32[:]).then_inc(SM["sVONE"], 1)

            def dveA_thunks(L):
                h, lt = L // NT, L % NT
                nk = NTK[lt] * 128
                nch = NCH[lt]
                mc = vf(O_MC + (L % 3) * 32, 8)
                th = []
                for ev in range(NEV[lt]):
                    def f(ev=ev):
                        ci0 = 2 * ev
                        two = ci0 + 1 < nch
                        lastci = ci0 + (1 if two else 0)
                        v.wait_ge(SM["sSMM"], GC(h, lt, lastci) + 1)
                        if ev == 0 and L >= 3:
                            v.wait_ge(SM["sEXP"], L - 2)
                        if two:
                            w2 = min(512, nk - (ci0 + 1) * 512)
                            srcp = AP(tensor=PSQ[0], offset=0,
                                      ap=[[1024, 128], [512, 2], [1, 512]])
                            dst = AP(tensor=arena,
                                     offset=(O_S2 + (L % 3) * 16384) // 4 + ci0 * 512,
                                     ap=[[PS_F32, 128], [1, 512 + w2]])
                            if w2 < 512:
                                srcp = AP(tensor=PSQ[0], offset=0,
                                          ap=[[1024, 128], [1, 512 + w2]])
                            v.tensor_scalar(out=dst, in0=srcp,
                                            scalar1=0.0, scalar2=None,
                                            op0=ALU.add, op1=ALU.max,
                                            accum_out=mc[:, ev:ev + 1]
                                            ).then_inc(SM["sSEV"], 1)
                        else:
                            w = min(512, nk - ci0 * 512)
                            dst = AP(tensor=arena,
                                     offset=(O_S2 + (L % 3) * 16384) // 4 + ci0 * 512,
                                     ap=[[PS_F32, 128], [1, w]])
                            v.tensor_scalar(out=dst, in0=psap(ci0 % 2, w),
                                            scalar1=0.0, scalar2=None,
                                            op0=ALU.add, op1=ALU.max,
                                            accum_out=mc[:, ev:ev + 1]
                                            ).then_inc(SM["sSEV"], 1)
                    th.append(f)

                def tail():
                    v.wait_ge(SM["sSEV"], GEV(h, lt) + NEV[lt])
                    v.reduce_max(out=vf(O_NM + (L % 3) * 16, 1),
                                 in_=mc[:, 0:NEV[lt]], axis=AX.X,
                                 negate=True).then_inc(SM["sNM"], 1)
                return th, tail

            def dveB_thunks(L):
                h, lt = L // NT, L % NT
                ntk = NTK[lt]
                th = {}
                first = [True]

                def mk(pr):
                    def f():
                        if first[0]:
                            first[0] = False
                            if L >= 2:
                                v.wait_ge(SM["sPV"], L - 1)
                        gp = GP(h, lt, pr)
                        v.wait_ge(SM["sPT"], gp + 1)
                        srcp = AP(tensor=PSQ[1 + gp % 2], offset=0,
                                  ap=[[1024, 128], [512, 2], [1, 128]])
                        dst = AP(tensor=arena_bf,
                                 offset=(O_PT2 + (L % 2) * 8192) // 2 + 2 * pr * 128,
                                 ap=[[PS_BF, 128], [1, 256]])
                        v.tensor_copy(out=dst, in_=srcp).then_inc(SM["sPTEVD"], 1)
                    return f
                for pr in range(ntk // 2):
                    gp = GP(h, lt, pr)
                    if gp % 2 == 0:
                        th[pr] = mk(pr)

                def tail():
                    if first[0] and L >= 2:
                        v.wait_ge(SM["sPV"], L - 1)
                    v.wait_ge(SM["sPV"], L + 1)
                    if L >= 2:
                        v.wait_ge(SM["sOEV"], L - 1)
                    v.reciprocal(out=vf(O_INV + (L % 2) * 4, 1),
                                 in_=AP(tensor=PSQ[3], offset=512 * (lt % 2) + 128,
                                        ap=[[1024, 128], [1, 1]])
                                 ).then_inc(SM["sINV"], 1)
                return th, tail

            for L in range(NH * NT + 2):
                if L < NH * NT:
                    ath, atail = dveA_thunks(L)
                    nch = NCH[L % NT]
                else:
                    ath, atail, nch = [], None, 0
                if L >= 2:
                    bmap, btail = dveB_thunks(L - 2)
                else:
                    bmap, btail = {}, None
                for tok in p2_master(L):
                    if tok[0] == 'sc':
                        ci = tok[1]
                        if ci % 2 == 1 or (ci == nch - 1 and nch % 2 == 1):
                            ath[ci // 2]()
                    elif tok[0] == 'pt' and tok[1] in bmap:
                        bmap[tok[1]]()
                    elif tok[0] == 'atail':
                        atail()
                    elif tok[0] == 'btail':
                        btail()
            v.wait_ge(SM["sINV"], NH * NT)
            bar(v, BAR_F)
            if upto < 6:
                return
            for lt in range(NT):
                for k in range(8):
                    q = 8 * lt + k
                    if q % 2 != 0:
                        continue
                    v.wait_ge(SM["sOT"], q + 1)
                    srcp = AP(tensor=PSQ[1 + q % 2], offset=0,
                              ap=[[1024, 128], [512, 2], [1, 128]])
                    dst = AP(tensor=arena_bf,
                             offset=(O_CXT + (lt % 2) * 4096) // 2 + 2 * k * 128,
                             ap=[[PS_BF, 128], [1, 256]])
                    v.tensor_copy(out=dst, in_=srcp).then_inc(SM["sOTEVD"], 1)
            v.wait_ge(SM["sOTEVD"], CNT_CX[0][8 * NT])
            bar(v, BAR_G)
            if upto < 7:
                return

            v.wait_ge(SM["sP4C"], 48)
            us = vf(O_US4, 44)
            for qr in range(4):
                for G_ in range(4):
                    for icl in range(11):
                        ic = 11 * G_ + icl
                        gi = 44 * qr + ic
                        v.wait_ge(SM["sGU"], gi + 1)
                        su = vf(O_SU4 + (gi % 2) * 2048, 512)
                        v.tensor_scalar_mul(out=su,
                                            in0=psap(2 + gi % 2, 512),
                                            scalar1=us[:, ic:ic + 1]
                                            ).then_inc(SM["sSU"], 1)
                        v.wait_ge(SM["sSG"], gi + 1)
                        Gl = 4 * qr + G_
                        if Gl >= 1 and icl == 0:
                            v.wait_ge(SM["sDNMM"], 16 * Gl)
                        v.tensor_mul(out=vb(O_HDN + icl * 1024, 512),
                                     in0=vf(O_SG4 + (gi % 2) * 2048, 512),
                                     in1=su).then_inc(SM["sHD4"], 1)
                    for hc in range(4):
                        for qt in range(4):
                            W_ = (4 * qr + G_) * 4 + hc
                            dn = W_ * 4 + qt
                            v.wait_ge(SM["sDNMM"], dn + 1)
                            acc = vf(O_ACC + qt * 8192 + hc * 2048, 512)
                            if G_ == 0:
                                v.tensor_copy(out=acc,
                                              in_=psap(4 + dn % 2, 512)
                                              ).then_inc(SM["sACC"], 1)
                            else:
                                v.tensor_add(out=acc, in0=acc,
                                             in1=psap(4 + dn % 2, 512)
                                             ).then_inc(SM["sACC"], 1)
                for qt in range(4):
                    cb = 4 * qr + qt
                    v.wait_ge(SM["sXT"], 16 * (cb + 1))
                    v.wait_ge(SM["sST"], 16 * (cb + 1))
                    acc = vf(O_ACC + qt * 8192, H)
                    tmp = vf(O_TMP, H)
                    v.tensor_mul(out=tmp, in0=acc, in1=vf(O_DS, H)
                                 ).then_inc(SM["sT1"], 1)
                    v.wait_ge(SM["sT2"], cb + 1)
                    v.tensor_mul(out=tmp, in0=vf(O_TMP2, H),
                                 in1=vf(O_STI if cb % 2 == 0 else O_STI2, H))
                    if cb >= 2:
                        v.wait_ge(SM["sOUT"], 16 * (cb - 1) + (96 if debug_taps else 0))
                    v.tensor_add(out=vf(O_OUTS + (cb % 2) * 8192, H),
                                 in0=tmp,
                                 in1=vf(O_XTI if cb % 2 == 0 else O_XTI2, H)
                                 ).then_inc(SM["sCMB"], 1)

        @block.scalar
        def _(sc):
            bar(sc, BAR_A)
            if upto < 1:
                return

            def ln_sc(n_tiles, xf_off, sq_off, st_off, names):
                sXFn, sSQn, sVARn, sSDn, sNBn, sXNn, sTRn, sTEVSn, xn_off, xnt_off, seqlen = names
                stats = vf(st_off, 16)
                for i in range(n_tiles):
                    sl = (i % 2) * 8
                    xf = vf(xf_off + (i % 2) * 8192, H)
                    sc.wait_ge(SM[sXFn[i % 2]], 16 * (i // 2 + 1))
                    if i >= 1:
                        sc.wait_ge(SM[sVARn], 3 * i - 1)
                    sc.activation(out=vf(sq_off, H), in_=xf, func=AF.Square
                                  ).then_inc(SM[sSQn], 1)
                    sc.wait_ge(SM[sVARn], 3 * (i + 1))
                    sc.activation(out=stats[:, sl + 7:sl + 8],
                                  in_=stats[:, sl + 5:sl + 6], func=AF.Sqrt
                                  ).then_inc(SM[sSDn], 1)
                    sc.wait_ge(SM[sNBn], i + 1)
                    if i >= 2:
                        sc.wait_ge(SM[sTRn], 8 * (i - 1))
                    sc.activation(out=vb(xn_off + (i % 2) * 4096, H), in_=xf,
                                  func=AF.Identity,
                                  bias=stats[:, sl + 7:sl + 8],
                                  scale=stats[:, sl + 6:sl + 7]
                                  ).then_inc(SM[sXNn], 1)
                    for k in range(8):
                        q = 8 * i + k
                        if q % 2 != 1:
                            continue
                        sc.wait_ge(SM[sTRn], q + 1)
                        srcp = AP(tensor=PSQ[1 + q % 2], offset=0,
                                  ap=[[1024, 128], [512, 2], [1, 128]])
                        dst = AP(tensor=arena_bf,
                                 offset=xnt_off // 2 + 2 * k * seqlen + i * 128,
                                 ap=[[PS_BF, 128], [seqlen, 2], [1, 128]])
                        sc.activation(out=dst, in_=srcp, func=AF.Copy
                                      ).then_inc(SM[sTEVSn], 1)

            ln_sc(NTF, A_XF, A_SQ1, A_ST,
                  (("sXF0", "sXF1"), "sSQ", "sVAR", "sSD", "sNB", "sXN", "sTR",
                   "sTEVS", A_XN2, O_XNTF, S))
            sc.wait_ge(SM["sTEVS"], CNT_TR[1][8 * NTF])
            sc.wait_ge(SM["sXN"], NTF)
            bar(sc, BAR_B)
            if upto < 2:
                return

            sc.wait_ge(SM["sKST"], 16 * 128)
            for vc in range(4):
                for tt in range(NTF):
                    vv = 32 * vc + tt
                    sc.wait_ge(SM["sVMM"], vv + 1)
                    if vv >= 3:
                        sc.wait_ge(SM["sVST"], 16 * (vv - 2))
                    sc.activation(out=vb(B_VO + (vv % 3) * 1024, 512),
                                  in_=psap(vv % 2, 512),
                                  func=AF.Copy).then_inc(SM["sVEV"], 1)
            sc.wait_ge(SM["sVEV"], 128)
            bar(sc, BAR_C)
            if upto < 3:
                return

            ln_sc(NT, C_XF, C_SQ, C_ST,
                  (("sXF2a", "sXF2b"), "sSQ2", "sVAR2", "sSD2", "sNB2", "sXN2",
                   "sTR2", "sTEVS2", C_XN, O_XNTM, TOK))
            sc.wait_ge(SM["sTEVS2"], CNT_TR2[1][8 * NT])
            sc.wait_ge(SM["sXN2"], NT)
            bar(sc, BAR_D)
            if upto < 4:
                return
            bar(sc, BAR_E)
            if upto < 5:
                return

            def sc_stageA(L):
                h, lt = L // NT, L % NT
                nk = NTK[lt] * 128
                sc.wait_ge(SM["sNM"], L + 1)
                if L >= 2:
                    sc.wait_ge(SM["sPT"], CUMT[L - 2] // 2)
                src = AP(tensor=arena, offset=(O_S2 + (L % 3) * 16384) // 4,
                         ap=[[PS_F32, 128], [1, nk]])
                dst = AP(tensor=arena_bf, offset=(O_P2 + (L % 2) * 8192) // 2,
                         ap=[[PS_BF, 128], [1, nk]])
                sc.activation(out=dst, in_=src, func=AF.Exp,
                              bias=vf(O_NM + (L % 3) * 16, 1)
                              ).then_inc(SM["sEXP"], 1)

            def sc_b_thunks(L):
                h, lt = L // NT, L % NT
                mp = {}
                first = [True]

                def mk(pr):
                    def f():
                        if first[0]:
                            first[0] = False
                            if L >= 2:
                                sc.wait_ge(SM["sPV"], L - 1)
                        gp = GP(h, lt, pr)
                        sc.wait_ge(SM["sPT"], gp + 1)
                        srcp = AP(tensor=PSQ[1 + gp % 2], offset=0,
                                  ap=[[1024, 128], [512, 2], [1, 128]])
                        dst = AP(tensor=arena_bf,
                                 offset=(O_PT2 + (L % 2) * 8192) // 2 + 2 * pr * 128,
                                 ap=[[PS_BF, 128], [1, 256]])
                        sc.activation(out=dst, in_=srcp, func=AF.Copy
                                      ).then_inc(SM["sPTEVS"], 1)
                    return f
                for pr in range(NTK[lt] // 2):
                    if GP(h, lt, pr) % 2 == 1:
                        mp[pr] = mk(pr)

                def oev():
                    sc.wait_ge(SM["sINV"], L + 1)
                    ctxd = AP(tensor=arena_bf,
                              offset=O_CTX // 2 + lt * 2048 + h * 128,
                              ap=[[PS_BF, 128], [1, 128]])
                    sc.activation(out=ctxd,
                                  in_=AP(tensor=PSQ[3], offset=512 * (lt % 2),
                                         ap=[[1024, 128], [1, 128]]),
                                  func=AF.Copy, scale=vf(O_INV + (L % 2) * 4, 1)
                                  ).then_inc(SM["sOEV"], 1)
                return mp, oev

            for L in range(NH * NT + 2):
                if L >= 2:
                    bmap, oev = sc_b_thunks(L - 2)
                else:
                    bmap, oev = {}, None
                for tok in p2_master(L):
                    if tok[0] == 'pt' and tok[1] in bmap:
                        bmap[tok[1]]()
                    elif tok[0] == 'oev':
                        oev()
                    elif tok[0] == 'exp':
                        sc_stageA(L)
            sc.wait_ge(SM["sOEV"], NH * NT)
            bar(sc, BAR_F)
            if upto < 6:
                return

            for lt in range(NT):
                for k in range(8):
                    q = 8 * lt + k
                    if q % 2 != 1:
                        continue
                    sc.wait_ge(SM["sOT"], q + 1)
                    srcp = AP(tensor=PSQ[1 + q % 2], offset=0,
                              ap=[[1024, 128], [512, 2], [1, 128]])
                    dst = AP(tensor=arena_bf,
                             offset=(O_CXT + (lt % 2) * 4096) // 2 + 2 * k * 128,
                             ap=[[PS_BF, 128], [1, 256]])
                    sc.activation(out=dst, in_=srcp, func=AF.Copy
                                  ).then_inc(SM["sOTEVS"], 1)
                for hc in range(4):
                    dd = 4 * lt + hc
                    sc.wait_ge(SM["sDMM"], dd + 1)
                    if dd >= 3:
                        sc.wait_ge(SM["sDST"], 16 * (dd - 2))
                    sc.activation(out=vf(O_SO3 + (dd % 3) * 2048, 512),
                                  in_=psap(dd % 2, 512),
                                  func=AF.Copy).then_inc(SM["sDEV"], 1)
            sc.wait_ge(SM["sDEV"], 64)
            bar(sc, BAR_G)
            if upto < 7:
                return

            sc.wait_ge(SM["sP4C"], 48)
            gs = vf(O_GS4, 44)

            def w4_issue(W_):
                if W_ >= 2:
                    sc.wait_ge(SM["sDNMM"], 4 * (W_ - 1))
                if W_ >= 1:
                    sc.wait_ge(SM["sW4"], 16 * W_)
                G_w, hc_w = (W_ // 4) % 4, W_ % 4
                src = AP(tensor=d_dTb,
                         offset=(G_w * 1408) * H + hc_w * 512,
                         ap=[[H, 128], [128 * H, 11], [1, 512]])
                sc.dma_start(out=vbd(O_W4 + (W_ % 2) * 11264,
                                     [[512, 11], [1, 512]]),
                             in_=src).then_inc(SM["sW4"], 16)

            nxtw = [0]
            for qr in range(4):
                for G_ in range(4):
                    B = 4 * qr + G_
                    while nxtw[0] <= min(4 * B + 1, 63):
                        w4_issue(nxtw[0])
                        nxtw[0] += 1
                    for icl in range(11):
                        ic = 11 * G_ + icl
                        gi = 44 * qr + ic
                        sc.wait_ge(SM["sGU"], gi + 1)
                        if gi >= 2:
                            sc.wait_ge(SM["sHD4"], gi - 1)
                        sc.activation(out=vf(O_SG4 + (gi % 2) * 2048, 512),
                                      in_=psap(gi % 2, 512),
                                      func=AF.Silu, scale=gs[:, ic:ic + 1]
                                      ).then_inc(SM["sSG"], 1)
                while nxtw[0] <= 16 * qr + 15:
                    w4_issue(nxtw[0])
                    nxtw[0] += 1
                for qt in range(4):
                    cb = 4 * qr + qt
                    sc.wait_ge(SM["sT1"], cb + 1)
                    sc.activation(out=vf(O_TMP2, H), in_=vf(O_TMP, H),
                                  func=AF.Silu).then_inc(SM["sT2"], 1)

    return nc


def _host_constants(parity):
    inv_freq = (1.0 / (10000.0 ** (np.arange(0, HD, 2, dtype=np.float32)
                                   / np.float32(HD)))).astype(np.float32)
    ang = (np.arange(S, dtype=np.float32)[:, None] * inv_freq[None, :]
           ).astype(np.float32)
    cos = np.cos(ang).astype(np.float32)
    sin = np.sin(ang).astype(np.float32)
    C = np.ascontiguousarray(np.concatenate([cos.T, cos.T], axis=0))
    Sg = np.ascontiguousarray(np.concatenate([-sin.T, sin.T], axis=0))

    scale = np.float32(1.0 / math.sqrt(HD))
    idx = ((np.arange(NT)[:, None] * 2 + parity) * 128
           + np.arange(128)[None, :]).reshape(-1)
    Cm = np.ascontiguousarray(C[:, idx] * scale).astype(np.float32)
    Sm = np.ascontiguousarray(Sg[:, idx] * scale).astype(np.float32)

    m = np.arange(4223)
    delta_true = (4095 - m) - 128 * (1 - parity)
    n = np.maximum(delta_true, 0).astype(np.int64)
    nf = np.maximum(n, 1).astype(np.float32)
    large = ((np.log(nf / np.float32(16)) / np.float32(math.log(8.0)))
             * np.float32(16)).astype(np.int32) + 16
    large = np.minimum(large, 31)
    bucket = np.where(n < 16, n, large).astype(np.int64)
    onehot = np.zeros((32, 4223), dtype=np.float32)
    onehot[bucket, np.arange(4223)] = 1.0
    maskv = np.where(delta_true < 0, np.float32(NEG), np.float32(0.0))
    maskv = np.ascontiguousarray(
        np.tile(maskv[None, :], (NH, 1)).astype(np.float32))
    return C, Sg, Cm, Sm, onehot, maskv


def make_in_maps(L0_attn_x, L0_attn_qkv_w, L0_attn_dense_w,
                 L0_attn_rel_attn_table, L0_ffn_gate_w, L0_ffn_up_w,
                 L0_ffn_down_w, L0_ffn_gate_s, L0_ffn_up_s, L0_ffn_down_s):
    x = np.ascontiguousarray(np.asarray(L0_attn_x, dtype=np.float32))
    qkv_w = np.ascontiguousarray(np.asarray(L0_attn_qkv_w, dtype=np.float32))
    dense_w = np.ascontiguousarray(np.asarray(L0_attn_dense_w, dtype=np.float32))
    rel = np.ascontiguousarray(np.asarray(L0_attn_rel_attn_table, dtype=np.float32))
    gate_w = np.asarray(L0_ffn_gate_w, dtype=np.float32)
    up_w = np.asarray(L0_ffn_up_w, dtype=np.float32)
    down_w = np.asarray(L0_ffn_down_w, dtype=np.float32)
    gate_s = np.asarray(L0_ffn_gate_s, dtype=np.float32)
    up_s = np.asarray(L0_ffn_up_s, dtype=np.float32)
    down_s = np.asarray(L0_ffn_down_s, dtype=np.float32)

    gateT = np.ascontiguousarray(gate_w.T)
    upT = np.ascontiguousarray(up_w.T)
    downT = np.ascontiguousarray(down_w.T)
    c31 = np.ascontiguousarray(rel[31, :].reshape(NH, 1))
    gate_sT = np.ascontiguousarray(gate_s.reshape(44, 128).T)
    up_sT = np.ascontiguousarray(up_s.reshape(44, 128).T)
    ds_bc = np.ascontiguousarray(np.tile(down_s[None, :], (128, 1)))
    ident = np.eye(128, dtype=ml_dtypes.bfloat16)

    consts = {p: _host_constants(p) for p in (0, 1)}
    in_maps = []
    for core in range(8):
        b, p = core // 2, core % 2
        C, Sg, Cm, Sm, onehot, maskv = consts[p]
        xb = x[b]
        x_my = np.ascontiguousarray(
            xb.reshape(NTF, 128, H)[p::2].reshape(TOK, H))
        in_maps.append({
            "x_full": xb, "x_my": x_my, "qkv_w": qkv_w, "dense_w": dense_w,
            "gateT": gateT, "upT": upT, "downT": downT, "rel": rel,
            "c31": c31, "onehot": onehot, "maskv": maskv,
            "ropeC": C, "ropeS": Sg, "ropeCm": Cm, "ropeSm": Sm,
            "ident": ident, "gate_sT": gate_sT, "up_sT": up_sT, "ds_bc": ds_bc,
        })
    return in_maps


def kernel(L0_attn_x, L0_attn_qkv_w, L0_attn_dense_w, L0_attn_rel_attn_table,
           L0_ffn_gate_w, L0_ffn_up_w, L0_ffn_down_w, L0_ffn_gate_s,
           L0_ffn_up_s, L0_ffn_down_s):
    in_maps = make_in_maps(L0_attn_x, L0_attn_qkv_w, L0_attn_dense_w,
                           L0_attn_rel_attn_table, L0_ffn_gate_w, L0_ffn_up_w,
                           L0_ffn_down_w, L0_ffn_gate_s, L0_ffn_up_s,
                           L0_ffn_down_s)
    nc = build_nc()
    res = run_bass_kernel_spmd(nc, in_maps, core_ids=list(range(8)))
    out = np.empty((B, S, H), dtype=np.float32)
    for core in range(8):
        b, p = core // 2, core % 2
        shard = res.results[core]["out"]
        out[b].reshape(NTF, 128, H)[p::2] = shard.reshape(NT, 128, H)
    return out


if __name__ == "__main__":
    nc = build_nc()
    print("instructions:", nc.n_instructions())


# version 11
# speedup vs baseline: 1.0207x; 1.0207x over previous
import contextlib
import math
import numpy as np
import ml_dtypes

import concourse.bass as bass
import concourse.mybir as mybir
from concourse.bass_utils import run_bass_kernel_spmd
from concourse.ap import AP

F32 = mybir.dt.float32
BF16 = mybir.dt.bfloat16
AF = mybir.ActivationFunctionType
ALU = mybir.AluOpType
AX = mybir.AxisListType

B = 4
S = 4096
H = 2048
NH = 16
HD = 128
IN = 5632
TOK = 2048
NT = 16
NTF = 32
NEG = -1.0e38
VSTRIDE = 132

NTK = [2 * lt + 2 for lt in range(NT)]
NCH = [(ntk * 128 + 511) // 512 for ntk in NTK]
NGR = [(ntk + 3) // 4 for ntk in NTK]
CH_BEFORE = [0] * NT
GR_BEFORE = [0] * NT
for _lt in range(1, NT):
    CH_BEFORE[_lt] = CH_BEFORE[_lt - 1] + NCH[_lt - 1]
    GR_BEFORE[_lt] = GR_BEFORE[_lt - 1] + NGR[_lt - 1]
TOT_CH = CH_BEFORE[-1] + NCH[-1]
TOT_GR = GR_BEFORE[-1] + NGR[-1]
BIAS_OFF = [(30 - 2 * lt) * 128 for lt in range(NT)]

TK_BEFORE = [0] * NT
for _lt in range(1, NT):
    TK_BEFORE[_lt] = TK_BEFORE[_lt - 1] + NTK[_lt - 1]
TOT_TK = TK_BEFORE[-1] + NTK[-1]

CUMT = []
_c = 0
for _h in range(NH):
    for _lt in range(NT):
        _c += NTK[_lt]
        CUMT.append(_c)


def parity_cum(n):
    ce = [[0] * (n + 1), [0] * (n + 1)]
    for i in range(n):
        ce[0][i + 1] = ce[0][i] + (1 if i % 2 == 0 else 0)
        ce[1][i + 1] = ce[1][i] + (1 if i % 2 == 1 else 0)
    return ce


CNT_T = parity_cum(NH * TOT_TK)
CNT_TP = parity_cum(NH * TOT_TK // 2)
CNT_TR = parity_cum(8 * NTF)
CNT_TR2 = parity_cum(8 * NT)
CNT_CX = parity_cum(8 * NT)
NEV = [(n + 1) // 2 for n in NCH]
EV_BEFORE = [0] * NT
for _lt in range(1, NT):
    EV_BEFORE[_lt] = EV_BEFORE[_lt - 1] + NEV[_lt - 1]
TOT_EV = EV_BEFORE[-1] + NEV[-1]


def GEV(h, lt):
    return h * TOT_EV + EV_BEFORE[lt]


def GP(h, lt, pr):
    return (h * TOT_TK + TK_BEFORE[lt]) // 2 + pr


def GT(h, lt, kt):
    return h * TOT_TK + TK_BEFORE[lt] + kt


def GC(h, lt, ci):
    return h * TOT_CH + CH_BEFORE[lt] + ci


def p2_sc_toks(L):
    lt = L % NT
    tk = []
    if lt == 0:
        tk.append(('ld', 0))
    tk += [('sc', ci) for ci in range(NCH[lt])]
    return tk


def p2_pv_toks(L):
    npr = NTK[L % NT] // 2
    tk = [('hd', 0), ('pt', 0)]
    for pr in range(1, npr):
        tk += [('pt', pr), ('pv', pr - 1)]
    tk.append(('pv', npr - 1))
    return tk


def p2_master(L):
    a = p2_sc_toks(L) if L < NH * NT else []
    b = p2_pv_toks(L - 2) if L >= 2 else []
    toks = []
    if not a:
        toks = list(b)
    else:
        ratio = max(1, len(b) // len(a)) if b else 0
        si = pi = 0
        while si < len(a) or pi < len(b):
            if si < len(a):
                toks.append(a[si])
                si += 1
            for _ in range(ratio):
                if pi < len(b):
                    toks.append(b[pi])
                    pi += 1
            if si >= len(a):
                while pi < len(b):
                    toks.append(b[pi])
                    pi += 1
    if L < NH * NT:
        toks.append(('atail', 0))
    if L >= 2:
        toks.append(('btail', 0))
        toks.append(('oev', 0))
    if L < NH * NT:
        toks.append(('exp', 0))
    return toks


ARENA_BYTES = 208896
O_XNTF = 0
O_C = 131072
O_SSGN = 147456
O_W1 = 163840
O_CMY = 102400
O_SMY = 110592
O_XNTM = 0
O_P1C = 65536
O_CTX = 0
O_K2 = 65536
O_V2 = 81920
O_Q2 = 98816
O_B2 = 107008
O_S2 = 123392
O_P2 = 172544
O_PT2 = 188928
O_MC = 205312
O_NM = 205408
O_INV = 205456
O_WD3 = 65536
O_CXT = 131072
O_SO3 = 139264
O_ACC = 0
O_XNH = 32768
O_HDN = 49152
O_WG = 60416
O_WU = 68608
O_W4 = 76800
O_DS = 99328
O_XTI = 107520
O_STI = 115712
O_TMP = 123904
O_TMP2 = 132096
O_OUTS = 140288
O_SG4 = 156672
O_SU4 = 160768
O_GS4 = 164864
O_US4 = 165056
O_XNH2 = 165888
O_XTI2 = 182272
O_STI2 = 190464
P0_OH = O_W1
P0_MK = O_W1 + 4096
P0_BT = O_W1 + 8192
P0_BV = O_W1 + 12288
A_XF = O_W1
A_SQ1 = O_W1 + 16384
A_XN2 = O_W1 + 24576
A_ST = O_W1 + 32768
B_WK = O_W1
B_KO = O_W1 + 8192
B_T1 = O_W1 + 11264
B_T2 = O_W1 + 13312
B_WV = O_W1
B_VO = O_W1 + 32768
B_WQ = O_W1
B_WQ2 = 118784
B_QO = O_W1 + 32768
Q_T1 = O_W1 + 35840
Q_T2 = O_W1 + 37888
C_XF = O_P1C
C_SQ = O_P1C + 16384
C_XN = O_P1C + 24576
C_ST = O_P1C + 32768
A_DG = A_ST + 256
A_XB3 = A_DG + 1024
C_DG = C_ST + 256
C_XB3 = C_DG + 1024


def build_nc(debug_taps=False, upto=7):
    nc = bass.Bass("TRN2", target_bir_lowering=False,
                   detect_race_conditions=False)

    d_xf = nc.dram_tensor("x_full", [S, H], F32, kind="ExternalInput")
    d_xm = nc.dram_tensor("x_my", [TOK, H], F32, kind="ExternalInput")
    d_qkv = nc.dram_tensor("qkv_w", [H, 3 * H], F32, kind="ExternalInput")
    d_dw = nc.dram_tensor("dense_w", [H, H], F32, kind="ExternalInput")
    d_gT = nc.dram_tensor("gateT", [H, IN], F32, kind="ExternalInput")
    d_uT = nc.dram_tensor("upT", [H, IN], F32, kind="ExternalInput")
    d_dT = nc.dram_tensor("downT", [IN, H], F32, kind="ExternalInput")
    d_rel = nc.dram_tensor("rel", [32, NH], F32, kind="ExternalInput")
    d_c31 = nc.dram_tensor("c31", [NH, 1], F32, kind="ExternalInput")
    d_oh = nc.dram_tensor("onehot", [32, 4223], F32, kind="ExternalInput")
    d_mk = nc.dram_tensor("maskv", [NH, 4223], F32, kind="ExternalInput")
    d_C = nc.dram_tensor("ropeC", [128, S], F32, kind="ExternalInput")
    d_Sg = nc.dram_tensor("ropeS", [128, S], F32, kind="ExternalInput")
    d_Cm = nc.dram_tensor("ropeCm", [128, TOK], F32, kind="ExternalInput")
    d_Sm = nc.dram_tensor("ropeSm", [128, TOK], F32, kind="ExternalInput")
    d_id = nc.dram_tensor("ident", [128, 128], BF16, kind="ExternalInput")
    d_gs = nc.dram_tensor("gate_sT", [128, 44], F32, kind="ExternalInput")
    d_us = nc.dram_tensor("up_sT", [128, 44], F32, kind="ExternalInput")
    d_ds = nc.dram_tensor("ds_bc", [128, H], F32, kind="ExternalInput")

    d_out = nc.dram_tensor("out", [TOK, H], F32, kind="ExternalOutput")

    d_qkvb = nc.dram_tensor("qkv_bf", [H, 3 * H], BF16, kind="Internal")
    d_dwb = nc.dram_tensor("dense_bf", [H, H], BF16, kind="Internal")
    d_gTb = nc.dram_tensor("gateT_bf", [H, IN], BF16, kind="Internal")
    d_uTb = nc.dram_tensor("upT_bf", [H, IN], BF16, kind="Internal")
    d_dTb = nc.dram_tensor("downT_bf", [IN, H], BF16, kind="Internal")
    d_brev = nc.dram_tensor("biasrev", [NH, 4223], F32, kind="Internal")
    d_bexp = nc.dram_tensor("biasexp", [NH * 128, S], BF16, kind="Internal")
    d_K = nc.dram_tensor("Kd", [NH, 128, S], BF16, kind="Internal")
    d_V = nc.dram_tensor("Vd", [S, H], BF16, kind="Internal")
    d_Q = nc.dram_tensor("Qd", [NH, 128, TOK], BF16, kind="Internal")
    d_xnm = nc.dram_tensor("xnTmy", [NH, 128, TOK], BF16, kind="Internal")
    d_s = nc.dram_tensor("sd", [TOK, H], F32, kind="Internal")

    taps = {}
    if debug_taps:
        for nm_, sh_, dt_ in [("t_xnm", [NH, 128, TOK], BF16),
                              ("t_K", [NH, 128, S], BF16),
                              ("t_V", [S, H], BF16),
                              ("t_Q", [NH, 128, TOK], BF16),
                              ("t_bexp", [NH * 128, S], BF16),
                              ("t_s", [TOK, H], F32)]:
            taps[nm_] = nc.dram_tensor(nm_, sh_, dt_, kind="ExternalOutput")

    es = contextlib.ExitStack()
    with es:
        arena = es.enter_context(
            nc.sbuf_tensor("arena", [128, ARENA_BYTES // 4], F32))
        ident = es.enter_context(nc.sbuf_tensor("identsb", [128, 128], BF16))
        ones32 = es.enter_context(nc.sbuf_tensor("ones32", [128, 32], BF16))
        ones128 = es.enter_context(nc.sbuf_tensor("ones128", [128, 128], BF16))
        rel_sb = es.enter_context(nc.sbuf_tensor("rel_sb", [32, NH], F32))
        c31_sb = es.enter_context(nc.sbuf_tensor("c31_sb", [NH, 1], F32))

        arena_bf = arena.bitcast(BF16)
        PS_F32 = arena.ap().ap[0][0]
        PS_BF = arena_bf.ap().ap[0][0]

        def vf(off_bytes, n):
            assert off_bytes % 4 == 0 and off_bytes + 4 * n <= ARENA_BYTES
            return AP(tensor=arena, offset=off_bytes // 4,
                      ap=[[PS_F32, 128], [1, n]])

        def vfd(off_bytes, dims):
            return AP(tensor=arena, offset=off_bytes // 4,
                      ap=[[PS_F32, 128]] + dims)

        def vb(off_bytes, n):
            assert off_bytes % 2 == 0 and off_bytes + 2 * n <= ARENA_BYTES
            return AP(tensor=arena_bf, offset=off_bytes // 2,
                      ap=[[PS_BF, 128], [1, n]])

        def vbd(off_bytes, dims):
            return AP(tensor=arena_bf, offset=off_bytes // 2,
                      ap=[[PS_BF, 128]] + dims)

        def vf16(off_bytes, n):
            return AP(tensor=arena, offset=off_bytes // 4,
                      ap=[[PS_F32, NH], [1, n]])

        def vf32p(off_bytes, n):
            return AP(tensor=arena, offset=off_bytes // 4,
                      ap=[[PS_F32, 32], [1, n]])

        sem_names = [
            "ph",
            "s0L", "sOH", "sMK", "sBMM", "sBEV", "sBV", "sBEXP",
            "sCK", "sCV", "sCQ", "sCD", "sCG", "sCU", "sC4",
            "sXF0", "sXF1", "sSQ", "sVAR", "sSD", "sNB", "sXN", "sTR", "sTEVD", "sTEVS",
            "sWK", "sKMM", "sKEV", "sKST",
            "sWV", "sVMM", "sVEV", "sVST",
            "sXF2a", "sXF2b", "sSQ2", "sVAR2", "sSD2", "sNB2", "sXN2", "sTR2", "sTEVD2",
            "sTEVS2", "sXMYST",
            "sWQ0", "sWQ1", "sQMM", "sQEV", "sQST",
            "sKL", "sVL", "sQL", "sBL", "sVONE", "sHD", "sSMM", "sSEV", "sMX",
            "sNM", "sEXP", "sPT", "sPTEVD", "sPTEVS", "sPV", "sINV", "sOEV",
            "sOT",
            "sWD", "sDMM", "sDEV", "sDST", "sOTEVD", "sOTEVS",
            "vch", "sDG", "sDG2", "sP4C", "sXH", "sWG", "sWU", "sW4", "sGU", "sSG", "sSU", "sHD4",
            "sDNMM", "sACC", "sXT", "sST", "sT1", "sT2", "sCMB", "sOUT",
        ]
        SM = {}
        for n in sem_names:
            SM[n] = es.enter_context(nc.semaphore(n))

        PSQ = [es.enter_context(nc.psum_tensor(f"psq{i}", [128, 1024], F32))
               for i in range(4)]

        def psap(i, w, off=0):
            tq = PSQ[i // 2]
            return AP(tensor=tq, offset=512 * (i % 2) + off,
                      ap=[[1024, 128], [1, w]])

        def pspair(i2, dims):
            return AP(tensor=PSQ[i2], offset=0, ap=[[1024, 128]] + dims)

        block = es.enter_context(nc.Block())

        NBARS = 7
        (BAR_A, BAR_B, BAR_C, BAR_D, BAR_E, BAR_F, BAR_G) = range(1, 8)

        def bar(eng, k):
            eng.nop().then_inc(SM["ph"], 1)
            eng.wait_ge(SM["ph"], 5 * k)

        @block.sync
        def _(sync):
            sync.dma_start(out=ident[:], in_=d_id.ap()).then_inc(SM["s0L"], 16)
            sync.dma_start(out=rel_sb[:], in_=d_rel.ap()).then_inc(SM["s0L"], 16)
            sync.dma_start(out=c31_sb[:], in_=d_c31.ap()).then_inc(SM["s0L"], 16)
            sync.dma_start(out=vf(O_C, S), in_=d_C.ap()).then_inc(SM["s0L"], 16)
            sync.dma_start(out=vf(O_SSGN, S), in_=d_Sg.ap()).then_inc(SM["s0L"], 16)
            for ci in range(9):
                w = min(512, 4223 - ci * 512)
                if ci >= 1:
                    sync.wait_ge(SM["sBMM"], ci)
                    sync.wait_ge(SM["sBEV"], 2 * ci)
                sync.dma_start(out=vf32p(P0_OH + (ci % 2) * 2048, w),
                               in_=d_oh.ap()[:, ci * 512:ci * 512 + w]
                               ).then_inc(SM["sOH"], 16)
                sync.dma_start(out=vf16(P0_MK + (ci % 2) * 2048, w),
                               in_=d_mk.ap()[:, ci * 512:ci * 512 + w]
                               ).then_inc(SM["sMK"], 16)
            bar(sync, BAR_A)
            if upto < 1:
                return

            for i in range(NTF):
                if i >= 2:
                    sync.wait_ge(SM["sXN"], i - 1)
                    sync.wait_ge(SM["sSQ"], i - 1)
                sync.dma_start(out=vf(A_XF + (i % 2) * 8192, H),
                               in_=d_xf.ap()[i * 128:(i + 1) * 128, :]
                               ).then_inc(SM["sXF0" if i % 2 == 0 else "sXF1"], 16)
            bar(sync, BAR_B)
            if upto < 2:
                return

            sync.wait_ge(SM["sCK"], 16)
            for c in range(NH):
                if c >= 2:
                    sync.wait_ge(SM["sKMM"], 8 * (c - 1))
                if c >= 1:
                    sync.wait_ge(SM["sWK"], 16 * c)
                src = AP(tensor=d_qkvb, offset=H + c * 128,
                         ap=[[3 * H, 128], [128 * 3 * H, NH], [1, 128]])
                sync.dma_start(out=vbd(B_WK + (c % 2) * 4096, [[128, NH], [1, 128]]),
                               in_=src).then_inc(SM["sWK"], 16)
            sync.wait_ge(SM["sCV"], 16)
            sync.wait_ge(SM["sKMM"], 128)
            sync.wait_ge(SM["sKST"], 16 * 128)
            for vc in range(4):
                if vc >= 2:
                    sync.wait_ge(SM["sVMM"], 32 * (vc - 1))
                if vc >= 1:
                    sync.wait_ge(SM["sWV"], 16 * vc)
                src = AP(tensor=d_qkvb, offset=2 * H + vc * 512,
                         ap=[[3 * H, 128], [128 * 3 * H, NH], [1, 512]])
                sync.dma_start(out=vbd(B_WV + (vc % 2) * 16384, [[512, NH], [1, 512]]),
                               in_=src).then_inc(SM["sWV"], 16)
            bar(sync, BAR_C)
            if upto < 3:
                return

            for j in range(NT):
                if j >= 2:
                    sync.wait_ge(SM["sXN2"], j - 1)
                    sync.wait_ge(SM["sSQ2"], j - 1)
                sync.dma_start(out=vf(C_XF + (j % 2) * 8192, H),
                               in_=d_xm.ap()[j * 128:(j + 1) * 128, :]
                               ).then_inc(SM["sXF2a" if j % 2 == 0 else "sXF2b"], 16)
            bar(sync, BAR_D)
            if upto < 4:
                return

            sync.dma_start(out=vf(O_CMY, TOK), in_=d_Cm.ap()).then_inc(SM["s0L"], 16)
            sync.dma_start(out=vf(O_SMY, TOK), in_=d_Sm.ap()).then_inc(SM["s0L"], 16)
            sync.wait_ge(SM["sCQ"], 16)
            for ch in range(2):
                src = AP(tensor=d_qkvb, offset=ch * 1024,
                         ap=[[3 * H, 128], [128 * 3 * H, NH], [1, 1024]])
                sync.dma_start(out=vbd(B_WQ if ch == 0 else B_WQ2,
                                       [[1024, NH], [1, 1024]]),
                               in_=src).then_inc(SM["sWQ0" if ch == 0
                                                 else "sWQ1"], 16)
            for h in range(2):
                if h == 1:
                    sync.wait_ge(SM["sKL"], 16)
                    sync.wait_ge(SM["sVL"], 16)
                sync.dma_start(out=vb(O_K2 + (h % 2) * 8192, S),
                               in_=AP(tensor=d_K, offset=h * 128 * S,
                                      ap=[[S, 128], [1, S]])
                               ).then_inc(SM["sKL"], 16)
                vdst = vbd(O_V2 + (h % 2) * 8448, [[VSTRIDE, 32], [1, 128]])
                vsrc = AP(tensor=d_V, offset=h * 128,
                          ap=[[H, 128], [128 * H, 32], [1, 128]])
                sync.dma_start(out=vdst, in_=vsrc).then_inc(SM["sVL"], 16)
            bar(sync, BAR_E)
            if upto < 5:
                return

            sync.wait_ge(SM["sVONE"], 2)
            sync.wait_ge(SM["sBEXP"], 16 * NH)
            for h in range(NH):
                if h >= 2:
                    sync.wait_ge(SM["sPV"], 16 * (h - 1))
                if h >= 1:
                    sync.wait_ge(SM["sQL"], 16 * h)
                    sync.wait_ge(SM["sBL"], 16 * h)
                if h >= 2:
                    sync.wait_ge(SM["sKL"], 16 * h)
                    sync.wait_ge(SM["sVL"], 16 * h)
                    sync.dma_start(out=vb(O_K2 + (h % 2) * 8192, S),
                                   in_=AP(tensor=d_K, offset=h * 128 * S,
                                          ap=[[S, 128], [1, S]])
                                   ).then_inc(SM["sKL"], 16)
                    vdst = vbd(O_V2 + (h % 2) * 8448,
                               [[VSTRIDE, 32], [1, 128]])
                    vsrc = AP(tensor=d_V, offset=h * 128,
                              ap=[[H, 128], [128 * H, 32], [1, 128]])
                    sync.dma_start(out=vdst, in_=vsrc).then_inc(SM["sVL"], 16)
                sync.dma_start(out=vb(O_Q2 + (h % 2) * 4096, TOK),
                               in_=AP(tensor=d_Q, offset=h * 128 * TOK,
                                      ap=[[TOK, 128], [1, TOK]])
                               ).then_inc(SM["sQL"], 16)
                sync.dma_start(out=vb(O_B2 + (h % 2) * 8192, S),
                               in_=AP(tensor=d_bexp, offset=h * 128 * S,
                                      ap=[[S, 128], [1, S]])
                               ).then_inc(SM["sBL"], 16)
            bar(sync, BAR_F)
            if upto < 6:
                return

            sync.wait_ge(SM["sCD"], 16)
            for hc in range(4):
                if hc >= 1:
                    sync.wait_ge(SM["sWD"], 16 * hc)
                src = AP(tensor=d_dwb, offset=hc * 512,
                         ap=[[H, 128], [128 * H, NH], [1, 512]])
                dstw = AP(tensor=arena_bf, offset=O_WD3 // 2 + hc * 512,
                          ap=[[PS_BF, 128], [2048, NH], [1, 512]])
                sync.dma_start(out=dstw, in_=src).then_inc(SM["sWD"], 16)
            bar(sync, BAR_G)
            if upto < 7:
                return

            sync.dma_start(out=vf(O_GS4, 44), in_=d_gs.ap()).then_inc(SM["sP4C"], 16)
            sync.dma_start(out=vf(O_US4, 44), in_=d_us.ap()).then_inc(SM["sP4C"], 16)
            sync.dma_start(out=vf(O_DS, H), in_=d_ds.ap()).then_inc(SM["sP4C"], 16)
            sync.wait_ge(SM["sCG"], 16)
            sync.wait_ge(SM["sCU"], 16)
            sync.wait_ge(SM["sC4"], 16)
            for qr in range(4):
                for G_ in range(4):
                    for icl in range(11):
                        ic = 11 * G_ + icl
                        gi = 44 * qr + ic
                        if gi >= 2:
                            sync.wait_ge(SM["sGU"], gi - 1)
                        if gi >= 1:
                            sync.wait_ge(SM["sWG"], 16 * gi)
                            sync.wait_ge(SM["sWU"], 16 * gi)
                        src = AP(tensor=d_gTb, offset=ic * 128,
                                 ap=[[IN, 128], [128 * IN, NH], [1, 128]])
                        sync.dma_start(out=vbd(O_WG + (gi % 2) * 4096,
                                               [[128, NH], [1, 128]]),
                                       in_=src).then_inc(SM["sWG"], 16)
                        src = AP(tensor=d_uTb, offset=ic * 128,
                                 ap=[[IN, 128], [128 * IN, NH], [1, 128]])
                        sync.dma_start(out=vbd(O_WU + (gi % 2) * 4096,
                                               [[128, NH], [1, 128]]),
                                       in_=src).then_inc(SM["sWU"], 16)
                    for hc in range(4):
                        W_ = (4 * qr + G_) * 4 + hc
                        if W_ >= 2:
                            sync.wait_ge(SM["sDNMM"], 4 * (W_ - 1))
                        if W_ >= 1:
                            sync.wait_ge(SM["sW4"], 16 * W_)
                        src = AP(tensor=d_dTb,
                                 offset=(G_ * 1408) * H + hc * 512,
                                 ap=[[H, 128], [128 * H, 11], [1, 512]])
                        sync.dma_start(out=vbd(O_W4 + (W_ % 2) * 11264,
                                               [[512, 11], [1, 512]]),
                                       in_=src).then_inc(SM["sW4"], 16)

        @block.gpsimd
        def _(g):
            g.dma_start(out=d_qkvb.ap()[:, H:2 * H],
                        in_=d_qkv.ap()[:, H:2 * H]).then_inc(SM["sCK"], 16)
            for ci in range(9):
                w = min(512, 4223 - ci * 512)
                g.wait_ge(SM["sBEV"], 2 * (ci + 1))
                if ci >= 1:
                    g.wait_ge(SM["sBV"], 16 * ci)
                g.dma_start(out=AP(tensor=d_brev, offset=ci * 512,
                                   ap=[[4223, NH], [1, w]]),
                            in_=vf16(P0_BV + (ci % 2) * 2048, w)
                            ).then_inc(SM["sBV"], 16)
            g.wait_ge(SM["sBV"], 16 * 9)
            for hh in range(NH):
                toe_src = AP(tensor=d_brev, offset=hh * 4223 + 127,
                             ap=[[128, 32], [-1, 128], [1, 128]])
                toe_dst = AP(tensor=d_bexp, offset=hh * 128 * S,
                             ap=[[128, 32], [S, 128], [1, 128]])
                g.dma_start(out=toe_dst, in_=toe_src).then_inc(SM["sBEXP"], 16)
            g.dma_start(out=d_qkvb.ap()[:, 2 * H:3 * H],
                        in_=d_qkv.ap()[:, 2 * H:3 * H]).then_inc(SM["sCV"], 16)
            g.dma_start(out=d_qkvb.ap()[:, 0:H],
                        in_=d_qkv.ap()[:, 0:H]).then_inc(SM["sCQ"], 16)
            bar(g, BAR_A)
            if upto < 1:
                return
            bar(g, BAR_B)
            if upto < 2:
                return

            for c in range(NH):
                for tc in range(8):
                    kk = 8 * c + tc
                    g.wait_ge(SM["sKEV"], kk + 1)
                    if kk >= 1:
                        g.wait_ge(SM["sKST"], 16 * kk)
                    dst = AP(tensor=d_K, offset=c * 128 * S + tc * 512,
                             ap=[[S, 128], [1, 512]])
                    g.dma_start(out=dst, in_=vb(B_KO + (kk % 3) * 1024, 512)
                                ).then_inc(SM["sKST"], 16)
            for vc in range(4):
                for tt in range(NTF):
                    vv = 32 * vc + tt
                    g.wait_ge(SM["sVEV"], vv + 1)
                    if vv >= 1:
                        g.wait_ge(SM["sVST"], 16 * vv)
                    dst = AP(tensor=d_V, offset=tt * 128 * H + vc * 512,
                             ap=[[H, 128], [1, 512]])
                    g.dma_start(out=dst, in_=vb(B_VO + (vv % 3) * 1024, 512)
                                ).then_inc(SM["sVST"], 16)
            g.wait_ge(SM["sKST"], 16 * 128)
            g.wait_ge(SM["sVST"], 16 * 128)
            bar(g, BAR_C)
            if upto < 3:
                return
            bar(g, BAR_D)
            if upto < 4:
                return

            src = vbd(O_XNTM, [[TOK, NH], [1, TOK]])
            g.dma_start(out=AP(tensor=d_xnm, offset=0,
                               ap=[[TOK, 128], [128 * TOK, NH], [1, TOK]]),
                        in_=src).then_inc(SM["sXMYST"], 16)
            for tc in range(4):
                for c in range(NH):
                    qq = 16 * tc + c
                    g.wait_ge(SM["sQEV"], qq + 1)
                    if qq >= 1:
                        g.wait_ge(SM["sQST"], 16 * qq)
                    dst = AP(tensor=d_Q, offset=c * 128 * TOK + tc * 512,
                             ap=[[TOK, 128], [1, 512]])
                    g.dma_start(out=dst, in_=vb(B_QO + (qq % 3) * 1024, 512)
                                ).then_inc(SM["sQST"], 16)
            g.wait_ge(SM["sQST"], 16 * 64)
            g.wait_ge(SM["sXMYST"], 16)
            if debug_taps:
                g.dma_start(out=taps["t_xnm"].ap(), in_=d_xnm.ap()).then_inc(SM["sOUT"], 16)
                g.dma_start(out=taps["t_K"].ap(), in_=d_K.ap()).then_inc(SM["sOUT"], 16)
                g.dma_start(out=taps["t_V"].ap(), in_=d_V.ap()).then_inc(SM["sOUT"], 16)
                g.dma_start(out=taps["t_Q"].ap(), in_=d_Q.ap()).then_inc(SM["sOUT"], 16)
                g.dma_start(out=taps["t_bexp"].ap(), in_=d_bexp.ap()).then_inc(SM["sOUT"], 16)
                g.wait_ge(SM["sOUT"], 16 * 5)
            bar(g, BAR_E)
            if upto < 5:
                return

            g.dma_start(out=d_dwb.ap(), in_=d_dw.ap()).then_inc(SM["sCD"], 16)
            g.dma_start(out=d_gTb.ap(), in_=d_gT.ap()).then_inc(SM["sCG"], 16)
            g.dma_start(out=d_uTb.ap(), in_=d_uT.ap()).then_inc(SM["sCU"], 16)
            g.dma_start(out=d_dTb.ap(), in_=d_dT.ap()).then_inc(SM["sC4"], 16)
            bar(g, BAR_F)
            if upto < 6:
                return

            for lt in range(NT):
                for hc in range(4):
                    dd = 4 * lt + hc
                    g.wait_ge(SM["sDEV"], dd + 1)
                    if dd >= 1:
                        g.wait_ge(SM["sDST"], 16 * dd)
                    dst = AP(tensor=d_s, offset=lt * 128 * H + hc * 512,
                             ap=[[H, 128], [1, 512]])
                    g.dma_start(out=dst, in_=vf(O_SO3 + (dd % 3) * 2048, 512)
                                ).then_inc(SM["sDST"], 16)
            g.wait_ge(SM["sDST"], 16 * 64)
            ntap = 6 if debug_taps else 0
            if debug_taps:
                g.dma_start(out=taps["t_s"].ap(), in_=d_s.ap()).then_inc(SM["sOUT"], 16)
                g.wait_ge(SM["sOUT"], 16 * 6)
            bar(g, BAR_G)
            if upto < 7:
                return

            def xnh_load(qr):
                src = AP(tensor=d_xnm, offset=qr * 512,
                         ap=[[TOK, 128], [128 * TOK, NH], [1, 512]])
                g.dma_start(out=vbd(O_XNH if qr % 2 == 0 else O_XNH2,
                                    [[512, NH], [1, 512]]),
                            in_=src).then_inc(SM["sXH"], 16)

            def xs_load(cb):
                row0 = cb * 128
                g.dma_start(out=vf(O_XTI if cb % 2 == 0 else O_XTI2, H),
                            in_=d_xm.ap()[row0:row0 + 128, :]
                            ).then_inc(SM["sXT"], 16)
                g.dma_start(out=vf(O_STI if cb % 2 == 0 else O_STI2, H),
                            in_=d_s.ap()[row0:row0 + 128, :]
                            ).then_inc(SM["sST"], 16)

            xnh_load(0)
            g.wait_ge(SM["sXH"], 16)
            xnh_load(1)
            xs_load(0)
            for cb in range(16):
                qr = cb // 4
                if cb % 4 == 0 and qr + 2 < 4:
                    g.wait_ge(SM["sXH"], 16 * (qr + 2))
                    g.wait_ge(SM["sGU"], 44 * (qr + 1))
                    xnh_load(qr + 2)
                if cb + 1 < 16:
                    g.wait_ge(SM["sXT"], 16 * (cb + 1))
                    g.wait_ge(SM["sST"], 16 * (cb + 1))
                    if cb >= 1:
                        g.wait_ge(SM["sCMB"], cb)
                    xs_load(cb + 1)
                g.wait_ge(SM["sCMB"], cb + 1)
                if cb >= 1:
                    g.wait_ge(SM["sOUT"], 16 * (cb + ntap))
                g.dma_start(out=d_out.ap()[cb * 128:(cb + 1) * 128, :],
                            in_=vf(O_OUTS + (cb % 2) * 8192, H)
                            ).then_inc(SM["sOUT"], 16)
            g.wait_ge(SM["sOUT"], 16 * (16 + ntap))

        @block.tensor
        def _(t):
            t.wait_ge(SM["s0L"], 16 * 5)
            for ci in range(9):
                w = min(512, 4223 - ci * 512)
                t.wait_ge(SM["sOH"], 16 * (ci + 1))
                if ci >= 2:
                    t.wait_ge(SM["sBEV"], 2 * (ci - 1))
                t.matmul(AP(tensor=PSQ[0], offset=512 * (ci % 2), ap=[[1024, NH], [1, w]]), rel_sb[:],
                         vf32p(P0_OH + (ci % 2) * 2048, w),
                         start=True, stop=True).then_inc(SM["sBMM"], 1)
            t.wait_ge(SM["sBMM"], 9)
            bar(t, BAR_A)
            if upto < 1:
                return

            def transposes(n_tiles, xb_off, xb3_off, dg_off, sTRn, sXNn,
                           sDGn, sTEVDn, sTEVSn, cnt):
                for i in range(n_tiles):
                    t.wait_ge(SM[sXNn], i + 1)
                    t.wait_ge(SM[sDGn], 2 * (i + 1))
                    xb = vb((xb_off + (i % 3) * 4096) if i % 3 < 2 else xb3_off,
                            H)
                    dgr = AP(tensor=arena_bf,
                             offset=(dg_off + (i % 2) * 256) // 2,
                             ap=[[PS_BF, 128], [1, 128]])
                    dgn = AP(tensor=arena_bf,
                             offset=(dg_off + 512 + (i % 2) * 256) // 2,
                             ap=[[PS_BF, 128], [1, 128]])
                    for k in range(8):
                        q = 8 * i + k
                        e = q % 2
                        sem = SM[sTEVDn] if e == 0 else SM[sTEVSn]
                        if q >= 2:
                            t.wait_ge(sem, cnt[e][q])
                        tq = PSQ[1 + q % 2]
                        for half in range(2):
                            hc = 2 * k + half
                            lhs = AP(tensor=arena_bf,
                                     offset=xb.offset + hc * 128,
                                     ap=[[PS_BF, 128], [1, 128]])
                            dst = AP(tensor=tq, offset=512 * half,
                                     ap=[[1024, 128], [1, 128]])
                            t.matmul(dst, lhs, dgr, start=True, stop=False)
                            ins = t.matmul(dst, ones128[:], dgn,
                                           start=False, stop=True)
                        ins.then_inc(SM[sTRn], 1)

            transposes(NTF, A_XN2, A_XB3, A_DG, "sTR", "sXN", "sDG",
                       "sTEVD", "sTEVS", CNT_TR)
            t.wait_ge(SM["sTR"], 8 * NTF)
            bar(t, BAR_B)
            if upto < 2:
                return

            for c in range(NH):
                t.wait_ge(SM["sWK"], 16 * (c + 1))
                for tc in range(8):
                    kk = 8 * c + tc
                    if kk >= 2:
                        t.wait_ge(SM["sKEV"], kk - 1)
                    for hc in range(NH):
                        lhs = AP(tensor=arena_bf,
                                 offset=(B_WK + (c % 2) * 4096) // 2 + hc * 128,
                                 ap=[[PS_BF, 128], [1, 128]])
                        rhs = AP(tensor=arena_bf,
                                 offset=O_XNTF // 2 + hc * S + tc * 512,
                                 ap=[[PS_BF, 128], [1, 512]])
                        ins = t.matmul(psap(kk % 2, 512), lhs, rhs,
                                       start=(hc == 0), stop=(hc == NH - 1))
                    ins.then_inc(SM["sKMM"], 1)
            for vc in range(4):
                t.wait_ge(SM["sWV"], 16 * (vc + 1))
                for tt in range(NTF):
                    vv = 32 * vc + tt
                    if vv >= 2:
                        t.wait_ge(SM["sVEV"], vv - 1)
                    else:
                        t.wait_ge(SM["sKEV"], 127 + vv)
                    for hc in range(NH):
                        lhs = AP(tensor=arena_bf,
                                 offset=O_XNTF // 2 + hc * S + tt * 128,
                                 ap=[[PS_BF, 128], [1, 128]])
                        rhs = AP(tensor=arena_bf,
                                 offset=(B_WV + (vc % 2) * 16384) // 2 + hc * 512,
                                 ap=[[PS_BF, 128], [1, 512]])
                        ins = t.matmul(psap(vv % 2, 512), lhs, rhs,
                                       start=(hc == 0), stop=(hc == NH - 1))
                    ins.then_inc(SM["sVMM"], 1)
            t.wait_ge(SM["sVMM"], 128)
            bar(t, BAR_C)
            if upto < 3:
                return

            transposes(NT, C_XN, C_XB3, C_DG, "sTR2", "sXN2", "sDG2",
                       "sTEVD2", "sTEVS2", CNT_TR2)
            t.wait_ge(SM["sTR2"], 8 * NT)
            bar(t, BAR_D)
            if upto < 4:
                return

            for idx in range(8):
                tc = idx // 2
                t.wait_ge(SM["sWQ0" if idx % 2 == 0 else "sWQ1"], 16)
                wqo = B_WQ if idx % 2 == 0 else B_WQ2
                for lc in range(8):
                    qq = 8 * idx + lc
                    if qq >= 2:
                        t.wait_ge(SM["sQEV"], qq - 1)
                    for hc in range(NH):
                        lhs = AP(tensor=arena_bf,
                                 offset=wqo // 2 + hc * 1024 + lc * 128,
                                 ap=[[PS_BF, 128], [1, 128]])
                        rhs = AP(tensor=arena_bf,
                                 offset=O_XNTM // 2 + hc * TOK + tc * 512,
                                 ap=[[PS_BF, 128], [1, 512]])
                        ins = t.matmul(psap(qq % 2, 512), lhs, rhs,
                                       start=(hc == 0), stop=(hc == NH - 1))
                    ins.then_inc(SM["sQMM"], 1)
            t.wait_ge(SM["sQMM"], 64)
            bar(t, BAR_E)
            if upto < 5:
                return

            def emit_scores(L):
                h, lt = L // NT, L % NT
                if lt == 0:
                    t.wait_ge(SM["sKL"], 16 * (h + 1))
                    t.wait_ge(SM["sQL"], 16 * (h + 1))
                    t.wait_ge(SM["sBL"], 16 * (h + 1))
                nk = NTK[lt] * 128
                for ci in range(NCH[lt]):
                    w = min(512, nk - ci * 512)
                    if ci < 2:
                        if L >= 1:
                            t.wait_ge(SM["sSEV"], GEV(L // NT, L % NT))
                    else:
                        t.wait_ge(SM["sSEV"], GEV(h, lt) + ci // 2)
                    lhs = AP(tensor=arena_bf,
                             offset=(O_Q2 + (h % 2) * 4096) // 2 + lt * 128,
                             ap=[[PS_BF, 128], [1, 128]])
                    rhs = AP(tensor=arena_bf,
                             offset=(O_K2 + (h % 2) * 8192) // 2 + ci * 512,
                             ap=[[PS_BF, 128], [1, w]])
                    has_bias = ci >= NCH[lt] - 2
                    ins = t.matmul(psap(ci % 2, w), lhs, rhs,
                                   start=True, stop=not has_bias)
                    if has_bias:
                        brhs = AP(tensor=arena_bf,
                                  offset=(O_B2 + (h % 2) * 8192) // 2
                                  + BIAS_OFF[lt] + ci * 512,
                                  ap=[[PS_BF, 128], [1, w]])
                        ins = t.matmul(psap(ci % 2, w), ident[:], brhs,
                                       start=False, stop=True)
                    ins.then_inc(SM["sSMM"], 1)

            def emit_ptpv(L):
                h, lt = L // NT, L % NT
                ntk = NTK[lt]

                def pt_pair(pr):
                    gp = GP(h, lt, pr)
                    e = gp % 2
                    sem = SM["sPTEVD"] if e == 0 else SM["sPTEVS"]
                    if gp >= 2:
                        t.wait_ge(sem, CNT_TP[e][gp])
                    tq = PSQ[1 + gp % 2]
                    for half in range(2):
                        kt = 2 * pr + half
                        lhs = AP(tensor=arena_bf,
                                 offset=(O_P2 + (L % 2) * 8192) // 2 + kt * 128,
                                 ap=[[PS_BF, 128], [1, 128]])
                        dst = AP(tensor=tq, offset=512 * half,
                                 ap=[[1024, 128], [1, 128]])
                        ins = t.matmul(dst, lhs, ident[:], start=True, stop=True)
                    ins.then_inc(SM["sPT"], 1)

                def pv_pair(pr):
                    gp = GP(h, lt, pr)
                    e = gp % 2
                    sem = SM["sPTEVD"] if e == 0 else SM["sPTEVS"]
                    t.wait_ge(sem, CNT_TP[e][gp + 1])
                    ins = None
                    for half in range(2):
                        kt = 2 * pr + half
                        lhs = AP(tensor=arena_bf,
                                 offset=(O_PT2 + (L % 2) * 8192) // 2 + kt * 128,
                                 ap=[[PS_BF, 128], [1, 128]])
                        rhs = AP(tensor=arena_bf,
                                 offset=(O_V2 + (h % 2) * 8448) // 2 + kt * VSTRIDE,
                                 ap=[[PS_BF, 128], [1, 129]])
                        odst = AP(tensor=PSQ[3], offset=512 * (lt % 2),
                                  ap=[[1024, 128], [1, 129]])
                        ins = t.matmul(odst, lhs, rhs,
                                       start=(kt == 0), stop=(kt == ntk - 1))
                    return ins

                t.wait_ge(SM["sEXP"], L + 1)
                if lt == 0:
                    t.wait_ge(SM["sVL"], 16 * (h + 1))
                if L >= 2:
                    t.wait_ge(SM["sOEV"], L - 1)
                npr = ntk // 2
                pt_pair(0)
                for pr in range(1, npr):
                    pt_pair(pr)
                    pv_pair(pr - 1)
                last = pv_pair(npr - 1)
                last.then_inc(SM["sPV"], 1)

            def scores_thunks(L):
                h, lt = L // NT, L % NT
                nk = NTK[lt] * 128
                th = []
                if lt == 0:
                    def t0():
                        t.wait_ge(SM["sKL"], 16 * (h + 1))
                        t.wait_ge(SM["sQL"], 16 * (h + 1))
                        t.wait_ge(SM["sBL"], 16 * (h + 1))
                    th.append(t0)
                for ci in range(NCH[lt]):
                    def tc(ci=ci):
                        w = min(512, nk - ci * 512)
                        if ci < 2:
                            if L >= 1:
                                t.wait_ge(SM["sSEV"], GEV(h, lt))
                        else:
                            t.wait_ge(SM["sSEV"], GEV(h, lt) + ci // 2)
                        lhs = AP(tensor=arena_bf,
                                 offset=(O_Q2 + (h % 2) * 4096) // 2 + lt * 128,
                                 ap=[[PS_BF, 128], [1, 128]])
                        rhs = AP(tensor=arena_bf,
                                 offset=(O_K2 + (h % 2) * 8192) // 2 + ci * 512,
                                 ap=[[PS_BF, 128], [1, w]])
                        has_bias = ci >= NCH[lt] - 2
                        ins = t.matmul(psap(ci % 2, w), lhs, rhs,
                                       start=True, stop=not has_bias)
                        if has_bias:
                            brhs = AP(tensor=arena_bf,
                                      offset=(O_B2 + (h % 2) * 8192) // 2
                                      + BIAS_OFF[lt] + ci * 512,
                                      ap=[[PS_BF, 128], [1, w]])
                            ins = t.matmul(psap(ci % 2, w), ident[:], brhs,
                                           start=False, stop=True)
                        ins.then_inc(SM["sSMM"], 1)
                    th.append(tc)
                return th

            def ptpv_thunks(L):
                h, lt = L // NT, L % NT
                ntk = NTK[lt]
                npr = ntk // 2
                th = []

                def mk_pt(pr):
                    def f():
                        gp = GP(h, lt, pr)
                        e = gp % 2
                        sem = SM["sPTEVD"] if e == 0 else SM["sPTEVS"]
                        if gp >= 2:
                            t.wait_ge(sem, CNT_TP[e][gp])
                        tq = PSQ[1 + gp % 2]
                        for half in range(2):
                            kt = 2 * pr + half
                            lhs = AP(tensor=arena_bf,
                                     offset=(O_P2 + (L % 2) * 8192) // 2 + kt * 128,
                                     ap=[[PS_BF, 128], [1, 128]])
                            dst = AP(tensor=tq, offset=512 * half,
                                     ap=[[1024, 128], [1, 128]])
                            ins = t.matmul(dst, lhs, ident[:],
                                           start=True, stop=True)
                        ins.then_inc(SM["sPT"], 1)
                    return f

                def mk_pv(pr):
                    def f():
                        gp = GP(h, lt, pr)
                        e = gp % 2
                        sem = SM["sPTEVD"] if e == 0 else SM["sPTEVS"]
                        t.wait_ge(sem, CNT_TP[e][gp + 1])
                        ins = None
                        for half in range(2):
                            kt = 2 * pr + half
                            lhs = AP(tensor=arena_bf,
                                     offset=(O_PT2 + (L % 2) * 8192) // 2 + kt * 128,
                                     ap=[[PS_BF, 128], [1, 128]])
                            rhs = AP(tensor=arena_bf,
                                     offset=(O_V2 + (h % 2) * 8448) // 2 + kt * VSTRIDE,
                                     ap=[[PS_BF, 128], [1, 129]])
                            odst = AP(tensor=PSQ[3], offset=512 * (lt % 2),
                                      ap=[[1024, 128], [1, 129]])
                            ins = t.matmul(odst, lhs, rhs,
                                           start=(kt == 0), stop=(kt == ntk - 1))
                        if pr == npr - 1:
                            ins.then_inc(SM["sPV"], 1)
                    return f

                def head():
                    t.wait_ge(SM["sEXP"], L + 1)
                    if lt == 0:
                        t.wait_ge(SM["sVL"], 16 * (h + 1))
                    if L >= 2:
                        t.wait_ge(SM["sOEV"], L - 1)
                th.append(head)
                th.append(mk_pt(0))
                for pr in range(1, npr):
                    th.append(mk_pt(pr))
                    th.append(mk_pv(pr - 1))
                th.append(mk_pv(npr - 1))
                return th

            for L in range(NH * NT + 2):
                sc_list = scores_thunks(L) if L < NH * NT else []
                pv_list = ptpv_thunks(L - 2) if L >= 2 else []
                si = pi = 0
                for tok in p2_master(L):
                    if tok[0] in ('ld', 'sc'):
                        sc_list[si]()
                        si += 1
                    elif tok[0] in ('hd', 'pt', 'pv'):
                        pv_list[pi]()
                        pi += 1
            t.wait_ge(SM["sPV"], NH * NT)
            bar(t, BAR_F)
            if upto < 6:
                return

            t.wait_ge(SM["sWD"], 16 * 4)
            for lt in range(NT):
                for k in range(8):
                    q = 8 * lt + k
                    e = q % 2
                    sem = SM["sOTEVD"] if e == 0 else SM["sOTEVS"]
                    if q >= 2:
                        t.wait_ge(sem, CNT_CX[e][q])
                    if k == 0 and lt >= 2:
                        t.wait_ge(SM["sDMM"], 4 * (lt - 1))
                    tq = PSQ[1 + q % 2]
                    for half in range(2):
                        c2 = 2 * k + half
                        lhs = AP(tensor=arena_bf,
                                 offset=O_CTX // 2 + lt * 2048 + c2 * 128,
                                 ap=[[PS_BF, 128], [1, 128]])
                        dstp = AP(tensor=tq, offset=512 * half,
                                  ap=[[1024, 128], [1, 128]])
                        ins = t.matmul(dstp, lhs, ident[:], start=True, stop=True)
                    ins.then_inc(SM["sOT"], 1)
                for hc in range(4):
                    dd = 4 * lt + hc
                    t.wait_ge(SM["sOTEVD"], CNT_CX[0][8 * lt + 8])
                    t.wait_ge(SM["sOTEVS"], CNT_CX[1][8 * lt + 8])
                    if dd >= 2:
                        t.wait_ge(SM["sDEV"], dd - 1)
                    for h2 in range(NH):
                        lhs = AP(tensor=arena_bf,
                                 offset=(O_CXT + (lt % 2) * 4096) // 2 + h2 * 128,
                                 ap=[[PS_BF, 128], [1, 128]])
                        rhs = AP(tensor=arena_bf,
                                 offset=O_WD3 // 2 + h2 * 2048 + hc * 512,
                                 ap=[[PS_BF, 128], [1, 512]])
                        ins = t.matmul(psap(dd % 2, 512), lhs, rhs,
                                       start=(h2 == 0), stop=(h2 == NH - 1))
                    ins.then_inc(SM["sDMM"], 1)
            t.wait_ge(SM["sDMM"], 64)
            bar(t, BAR_G)
            if upto < 7:
                return

            for qr in range(4):
                t.wait_ge(SM["sXH"], 16 * (qr + 1))
                for G_ in range(4):
                    for icl in range(11):
                        ic = 11 * G_ + icl
                        gi = 44 * qr + ic
                        t.wait_ge(SM["sWG"], 16 * (gi + 1))
                        t.wait_ge(SM["sWU"], 16 * (gi + 1))
                        if gi >= 2:
                            t.wait_ge(SM["sSG"], gi - 1)
                            t.wait_ge(SM["sSU"], gi - 1)
                        xnho = O_XNH if qr % 2 == 0 else O_XNH2
                        for hc in range(NH):
                            lhs = AP(tensor=arena_bf,
                                     offset=(O_WG + (gi % 2) * 4096) // 2 + hc * 128,
                                     ap=[[PS_BF, 128], [1, 128]])
                            rhs = AP(tensor=arena_bf,
                                     offset=xnho // 2 + hc * 512,
                                     ap=[[PS_BF, 128], [1, 512]])
                            t.matmul(psap(gi % 2, 512), lhs, rhs,
                                     start=(hc == 0), stop=(hc == NH - 1))
                        for hc in range(NH):
                            lhs = AP(tensor=arena_bf,
                                     offset=(O_WU + (gi % 2) * 4096) // 2 + hc * 128,
                                     ap=[[PS_BF, 128], [1, 128]])
                            rhs = AP(tensor=arena_bf,
                                     offset=xnho // 2 + hc * 512,
                                     ap=[[PS_BF, 128], [1, 512]])
                            ins = t.matmul(psap(2 + gi % 2, 512), lhs, rhs,
                                           start=(hc == 0), stop=(hc == NH - 1))
                        ins.then_inc(SM["sGU"], 1)
                    t.wait_ge(SM["sHD4"], 11 * (4 * qr + G_ + 1))
                    for hc in range(4):
                        W_ = (4 * qr + G_) * 4 + hc
                        t.wait_ge(SM["sW4"], 16 * (W_ + 1))
                        for qt in range(4):
                            dn = W_ * 4 + qt
                            if dn >= 2:
                                t.wait_ge(SM["sACC"], dn - 1)
                            for icl in range(11):
                                lhs = AP(tensor=arena_bf,
                                         offset=O_HDN // 2 + icl * 512 + qt * 128,
                                         ap=[[PS_BF, 128], [1, 128]])
                                rhs = AP(tensor=arena_bf,
                                         offset=(O_W4 + (W_ % 2) * 11264) // 2
                                         + icl * 512,
                                         ap=[[PS_BF, 128], [1, 512]])
                                ins = t.matmul(psap(4 + dn % 2, 512), lhs, rhs,
                                               start=(icl == 0), stop=(icl == 10))
                            ins.then_inc(SM["sDNMM"], 1)

        @block.vector
        def _(v):
            VCH = [0]
            v.memset(ones128[:], 1.0)
            v.wait_ge(SM["s0L"], 16 * 5)
            for ci in range(9):
                w = min(512, 4223 - ci * 512)
                v.wait_ge(SM["sBMM"], ci + 1)
                v.wait_ge(SM["sMK"], 16 * (ci + 1))
                if ci >= 2:
                    v.wait_ge(SM["sBV"], 16 * (ci - 1))
                bt = vf16(P0_BT + (ci % 2) * 2048, w)
                v.tensor_scalar(out=bt, in0=AP(tensor=PSQ[0], offset=512 * (ci % 2), ap=[[1024, NH], [1, w]]),
                                scalar1=c31_sb[:, 0:1], scalar2=None,
                                op0=ALU.subtract)
                v.tensor_tensor(out=vf16(P0_BV + (ci % 2) * 2048, w), in0=bt,
                                in1=vf16(P0_MK + (ci % 2) * 2048, w),
                                op=ALU.add).then_inc(SM["sBEV"], 2)
            v.wait_ge(SM["sBEV"], 18)
            bar(v, BAR_A)
            if upto < 1:
                return

            def ln_dve(n_tiles, xf_off, st_off, names):
                (sXFn, sSQn, sVARn, sSDn, sXNn, sTRn, sTEVDn, xb_off,
                 xb3_off, dg_off, sDGn, xnt_off, seqlen) = names
                stats = vf(st_off, 32)

                def xbslot(j):
                    return (xb_off + (j % 3) * 4096) if j % 3 < 2 else xb3_off

                for it in range(n_tiles + 2):
                    a = it - 1
                    if 0 <= a <= n_tiles - 1:
                        sl = (a % 2) * 10
                        v.wait_ge(SM[sXNn], a + 1)
                        v.wait_ge(SM[sSQn], a + 1)
                        vc = VCH[0]
                        v.tensor_scalar_mul(out=stats[:, sl + 2:sl + 3],
                                            in0=stats[:, sl:sl + 1],
                                            scalar1=1.0 / H
                                            ).then_inc(SM["vch"], 1)
                        v.tensor_scalar(out=stats[:, sl + 3:sl + 4],
                                        in0=stats[:, sl + 1:sl + 2],
                                        scalar1=1.0 / H, scalar2=1e-5,
                                        op0=ALU.mult, op1=ALU.add
                                        ).then_inc(SM["vch"], 1)
                        v.wait_ge(SM["vch"], vc + 2)
                        v.tensor_mul(out=stats[:, sl + 4:sl + 5],
                                     in0=stats[:, sl + 2:sl + 3],
                                     in1=stats[:, sl + 2:sl + 3]
                                     ).then_inc(SM["vch"], 1)
                        v.wait_ge(SM["vch"], vc + 3)
                        v.tensor_sub(out=stats[:, sl + 5:sl + 6],
                                     in0=stats[:, sl + 3:sl + 4],
                                     in1=stats[:, sl + 4:sl + 5]
                                     ).then_inc(SM[sVARn], 1)
                        VCH[0] += 3
                    b = it - 2
                    if 0 <= b <= n_tiles - 1:
                        sl = (b % 2) * 10
                        v.wait_ge(SM[sSDn], b + 1)
                        if b >= 2:
                            v.wait_ge(SM[sTRn], 8 * (b - 1))
                        vc = VCH[0]
                        v.reciprocal(out=stats[:, sl + 7:sl + 8],
                                     in_=stats[:, sl + 6:sl + 7]
                                     ).then_inc(SM["vch"], 1)
                        v.wait_ge(SM["vch"], vc + 1)
                        v.tensor_scalar(out=stats[:, sl + 8:sl + 9],
                                        in0=stats[:, sl + 2:sl + 3],
                                        scalar1=-1.0,
                                        scalar2=stats[:, sl + 7:sl + 8],
                                        op0=ALU.mult, op1=ALU.mult
                                        ).then_inc(SM["vch"], 1)
                        v.wait_ge(SM["vch"], vc + 2)
                        v.tensor_scalar_mul(out=vb(dg_off + (b % 2) * 256, 128),
                                            in0=ident[:],
                                            scalar1=stats[:, sl + 7:sl + 8]
                                            ).then_inc(SM[sDGn], 1)
                        v.tensor_scalar_mul(out=vb(dg_off + 512 + (b % 2) * 256,
                                                   128),
                                            in0=ident[:],
                                            scalar1=stats[:, sl + 8:sl + 9]
                                            ).then_inc(SM[sDGn], 1)
                        VCH[0] += 2
                    e_ = it - 2
                    if 0 <= e_ <= n_tiles - 1:
                        for k in range(8):
                            q = 8 * e_ + k
                            if q % 2 != 0:
                                continue
                            v.wait_ge(SM[sTRn], q + 1)
                            srcp = AP(tensor=PSQ[1 + q % 2], offset=0,
                                      ap=[[1024, 128], [512, 2], [1, 128]])
                            dst = AP(tensor=arena_bf,
                                     offset=xnt_off // 2 + 2 * k * seqlen
                                     + e_ * 128,
                                     ap=[[PS_BF, 128], [seqlen, 2], [1, 128]])
                            v.tensor_copy(out=dst, in_=srcp
                                          ).then_inc(SM[sTEVDn], 1)
                    c_ = it
                    if 0 <= c_ <= n_tiles - 1:
                        sl = (c_ % 2) * 10
                        v.wait_ge(SM[sXFn[c_ % 2]], 16 * (c_ // 2 + 1))
                        if c_ >= 3:
                            v.wait_ge(SM[sTRn], 8 * (c_ - 2))
                        v.tensor_scalar(out=vb(xbslot(c_), H),
                                        in0=vf(xf_off + (c_ % 2) * 8192, H),
                                        scalar1=0.0, scalar2=None,
                                        op0=ALU.add, op1=ALU.add,
                                        accum_out=stats[:, sl:sl + 1]
                                        ).then_inc(SM[sXNn], 1)

            ln_dve(NTF, A_XF, A_ST,
                   (("sXF0", "sXF1"), "sSQ", "sVAR", "sSD", "sXN", "sTR",
                    "sTEVD", A_XN2, A_XB3, A_DG, "sDG", O_XNTF, S))
            bar(v, BAR_B)
            if upto < 2:
                return

            def rope_dve(pairs, Coff, Soff, ko_off, sMMn, sEVn, sSTn,
                         t1off=B_T1, t2off=B_T2):
                for idx, tc in pairs:
                    v.wait_ge(SM[sMMn], idx + 1)
                    if idx >= 3:
                        v.wait_ge(SM[sSTn], 16 * (idx - 2))
                    ps = psap(idx % 2, 512)
                    t1 = vf(t1off, 512)
                    t2 = vf(t2off, 512)
                    Cs = vf(Coff + tc * 2048, 512)
                    Ss = vf(Soff + tc * 2048, 512)
                    v.tensor_mul(out=t1, in0=ps, in1=Cs)
                    v.tensor_mul(out=t2[0:64, :], in0=ps[64:128, :],
                                 in1=Ss[0:64, :])
                    v.tensor_mul(out=t2[64:128, :], in0=ps[0:64, :],
                                 in1=Ss[64:128, :])
                    v.tensor_add(out=vb(ko_off + (idx % 3) * 1024, 512),
                                 in0=t1, in1=t2).then_inc(SM[sEVn], 1)

            rope_dve([(8 * c + tc, tc) for c in range(NH) for tc in range(8)],
                     O_C, O_SSGN, B_KO, "sKMM", "sKEV", "sKST")
            v.wait_ge(SM["sKEV"], 128)
            bar(v, BAR_C)
            if upto < 3:
                return

            ln_dve(NT, C_XF, C_ST,
                   (("sXF2a", "sXF2b"), "sSQ2", "sVAR2", "sSD2", "sXN2",
                    "sTR2", "sTEVD2", C_XN, C_XB3, C_DG, "sDG2", O_XNTM, TOK))
            v.wait_ge(SM["sTEVD2"], CNT_TR2[0][8 * NT])
            bar(v, BAR_D)
            if upto < 4:
                return

            v.wait_ge(SM["s0L"], 16 * 7)
            rope_dve([(8 * idx + lc, idx // 2) for idx in range(8)
                      for lc in range(8)],
                     O_CMY, O_SMY, B_QO, "sQMM", "sQEV", "sQST",
                     t1off=Q_T1, t2off=Q_T2)
            v.wait_ge(SM["sQEV"], 64)
            bar(v, BAR_E)
            if upto < 5:
                return

            v.memset(ones32[:], 1.0)
            for hb in range(2):
                vdst = AP(tensor=arena_bf,
                          offset=(O_V2 + hb * 8448) // 2 + 128,
                          ap=[[PS_BF, 128], [VSTRIDE, 32]])
                v.tensor_copy(out=vdst, in_=ones32[:]).then_inc(SM["sVONE"], 1)

            def dveA_thunks(L):
                h, lt = L // NT, L % NT
                nk = NTK[lt] * 128
                nch = NCH[lt]
                mc = vf(O_MC + (L % 3) * 32, 8)
                th = []
                for ev in range(NEV[lt]):
                    def f(ev=ev):
                        ci0 = 2 * ev
                        two = ci0 + 1 < nch
                        lastci = ci0 + (1 if two else 0)
                        v.wait_ge(SM["sSMM"], GC(h, lt, lastci) + 1)
                        if ev == 0 and L >= 3:
                            v.wait_ge(SM["sEXP"], L - 2)
                        if two:
                            w2 = min(512, nk - (ci0 + 1) * 512)
                            srcp = AP(tensor=PSQ[0], offset=0,
                                      ap=[[1024, 128], [512, 2], [1, 512]])
                            dst = AP(tensor=arena,
                                     offset=(O_S2 + (L % 3) * 16384) // 4 + ci0 * 512,
                                     ap=[[PS_F32, 128], [1, 512 + w2]])
                            if w2 < 512:
                                srcp = AP(tensor=PSQ[0], offset=0,
                                          ap=[[1024, 128], [1, 512 + w2]])
                            v.tensor_scalar(out=dst, in0=srcp,
                                            scalar1=0.0, scalar2=None,
                                            op0=ALU.add, op1=ALU.max,
                                            accum_out=mc[:, ev:ev + 1]
                                            ).then_inc(SM["sSEV"], 1)
                        else:
                            w = min(512, nk - ci0 * 512)
                            dst = AP(tensor=arena,
                                     offset=(O_S2 + (L % 3) * 16384) // 4 + ci0 * 512,
                                     ap=[[PS_F32, 128], [1, w]])
                            v.tensor_scalar(out=dst, in0=psap(ci0 % 2, w),
                                            scalar1=0.0, scalar2=None,
                                            op0=ALU.add, op1=ALU.max,
                                            accum_out=mc[:, ev:ev + 1]
                                            ).then_inc(SM["sSEV"], 1)
                    th.append(f)

                def tail():
                    v.wait_ge(SM["sSEV"], GEV(h, lt) + NEV[lt])
                    v.reduce_max(out=vf(O_NM + (L % 3) * 16, 1),
                                 in_=mc[:, 0:NEV[lt]], axis=AX.X,
                                 negate=True).then_inc(SM["sNM"], 1)
                return th, tail

            def dveB_thunks(L):
                h, lt = L // NT, L % NT
                ntk = NTK[lt]
                th = {}
                first = [True]

                def mk(pr):
                    def f():
                        if first[0]:
                            first[0] = False
                            if L >= 2:
                                v.wait_ge(SM["sPV"], L - 1)
                        gp = GP(h, lt, pr)
                        v.wait_ge(SM["sPT"], gp + 1)
                        srcp = AP(tensor=PSQ[1 + gp % 2], offset=0,
                                  ap=[[1024, 128], [512, 2], [1, 128]])
                        dst = AP(tensor=arena_bf,
                                 offset=(O_PT2 + (L % 2) * 8192) // 2 + 2 * pr * 128,
                                 ap=[[PS_BF, 128], [1, 256]])
                        v.tensor_copy(out=dst, in_=srcp).then_inc(SM["sPTEVD"], 1)
                    return f
                for pr in range(ntk // 2):
                    gp = GP(h, lt, pr)
                    if gp % 2 == 0:
                        th[pr] = mk(pr)

                def tail():
                    if first[0] and L >= 2:
                        v.wait_ge(SM["sPV"], L - 1)
                    v.wait_ge(SM["sPV"], L + 1)
                    if L >= 2:
                        v.wait_ge(SM["sOEV"], L - 1)
                    v.reciprocal(out=vf(O_INV + (L % 2) * 4, 1),
                                 in_=AP(tensor=PSQ[3], offset=512 * (lt % 2) + 128,
                                        ap=[[1024, 128], [1, 1]])
                                 ).then_inc(SM["sINV"], 1)
                return th, tail

            for L in range(NH * NT + 2):
                if L < NH * NT:
                    ath, atail = dveA_thunks(L)
                    nch = NCH[L % NT]
                else:
                    ath, atail, nch = [], None, 0
                if L >= 2:
                    bmap, btail = dveB_thunks(L - 2)
                else:
                    bmap, btail = {}, None
                for tok in p2_master(L):
                    if tok[0] == 'sc':
                        ci = tok[1]
                        if ci % 2 == 1 or (ci == nch - 1 and nch % 2 == 1):
                            ath[ci // 2]()
                    elif tok[0] == 'pt' and tok[1] in bmap:
                        bmap[tok[1]]()
                    elif tok[0] == 'atail':
                        atail()
                    elif tok[0] == 'btail':
                        btail()
            v.wait_ge(SM["sINV"], NH * NT)
            bar(v, BAR_F)
            if upto < 6:
                return
            for lt in range(NT):
                for k in range(8):
                    q = 8 * lt + k
                    if q % 2 != 0:
                        continue
                    v.wait_ge(SM["sOT"], q + 1)
                    srcp = AP(tensor=PSQ[1 + q % 2], offset=0,
                              ap=[[1024, 128], [512, 2], [1, 128]])
                    dst = AP(tensor=arena_bf,
                             offset=(O_CXT + (lt % 2) * 4096) // 2 + 2 * k * 128,
                             ap=[[PS_BF, 128], [1, 256]])
                    v.tensor_copy(out=dst, in_=srcp).then_inc(SM["sOTEVD"], 1)
            v.wait_ge(SM["sOTEVD"], CNT_CX[0][8 * NT])
            bar(v, BAR_G)
            if upto < 7:
                return

            v.wait_ge(SM["sP4C"], 48)
            us = vf(O_US4, 44)
            for qr in range(4):
                for G_ in range(4):
                    for icl in range(11):
                        ic = 11 * G_ + icl
                        gi = 44 * qr + ic
                        v.wait_ge(SM["sGU"], gi + 1)
                        su = vf(O_SU4 + (gi % 2) * 2048, 512)
                        v.tensor_scalar_mul(out=su,
                                            in0=psap(2 + gi % 2, 512),
                                            scalar1=us[:, ic:ic + 1]
                                            ).then_inc(SM["sSU"], 1)
                        v.wait_ge(SM["sSG"], gi + 1)
                        Gl = 4 * qr + G_
                        if Gl >= 1 and icl == 0:
                            v.wait_ge(SM["sDNMM"], 16 * Gl)
                        v.tensor_mul(out=vb(O_HDN + icl * 1024, 512),
                                     in0=vf(O_SG4 + (gi % 2) * 2048, 512),
                                     in1=su).then_inc(SM["sHD4"], 1)
                    for hc in range(4):
                        for qt in range(4):
                            W_ = (4 * qr + G_) * 4 + hc
                            dn = W_ * 4 + qt
                            v.wait_ge(SM["sDNMM"], dn + 1)
                            acc = vf(O_ACC + qt * 8192 + hc * 2048, 512)
                            if G_ == 0:
                                v.tensor_copy(out=acc,
                                              in_=psap(4 + dn % 2, 512)
                                              ).then_inc(SM["sACC"], 1)
                            else:
                                v.tensor_add(out=acc, in0=acc,
                                             in1=psap(4 + dn % 2, 512)
                                             ).then_inc(SM["sACC"], 1)
                for qt in range(4):
                    cb = 4 * qr + qt
                    v.wait_ge(SM["sXT"], 16 * (cb + 1))
                    v.wait_ge(SM["sST"], 16 * (cb + 1))
                    acc = vf(O_ACC + qt * 8192, H)
                    tmp = vf(O_TMP, H)
                    v.tensor_mul(out=tmp, in0=acc, in1=vf(O_DS, H)
                                 ).then_inc(SM["sT1"], 1)
                    v.wait_ge(SM["sT2"], cb + 1)
                    v.tensor_mul(out=tmp, in0=vf(O_TMP2, H),
                                 in1=vf(O_STI if cb % 2 == 0 else O_STI2, H))
                    if cb >= 2:
                        v.wait_ge(SM["sOUT"], 16 * (cb - 1) + (96 if debug_taps else 0))
                    v.tensor_add(out=vf(O_OUTS + (cb % 2) * 8192, H),
                                 in0=tmp,
                                 in1=vf(O_XTI if cb % 2 == 0 else O_XTI2, H)
                                 ).then_inc(SM["sCMB"], 1)

        @block.scalar
        def _(sc):
            bar(sc, BAR_A)
            if upto < 1:
                return

            def ln_sc(n_tiles, xf_off, sq_off, st_off, names):
                (sXFn, sSQn, sVARn, sSDn, sTRn, sTEVSn, xnt_off,
                 seqlen) = names
                stats = vf(st_off, 32)
                for it in range(n_tiles + 2):
                    j = it
                    if 0 <= j <= n_tiles - 1:
                        sl = (j % 2) * 10
                        sc.wait_ge(SM[sXFn[j % 2]], 16 * (j // 2 + 1))
                        sc.activation(out=vf(sq_off, H),
                                      in_=vf(xf_off + (j % 2) * 8192, H),
                                      func=AF.Square,
                                      accum_out=stats[:, sl + 1:sl + 2]
                                      ).then_inc(SM[sSQn], 1)
                    a = it - 1
                    if 0 <= a <= n_tiles - 1:
                        sl = (a % 2) * 10
                        sc.wait_ge(SM[sVARn], a + 1)
                        sc.activation(out=stats[:, sl + 6:sl + 7],
                                      in_=stats[:, sl + 5:sl + 6],
                                      func=AF.Sqrt).then_inc(SM[sSDn], 1)
                    e_ = it - 2
                    if 0 <= e_ <= n_tiles - 1:
                        for k in range(8):
                            q = 8 * e_ + k
                            if q % 2 != 1:
                                continue
                            sc.wait_ge(SM[sTRn], q + 1)
                            srcp = AP(tensor=PSQ[1 + q % 2], offset=0,
                                      ap=[[1024, 128], [512, 2], [1, 128]])
                            dst = AP(tensor=arena_bf,
                                     offset=xnt_off // 2 + 2 * k * seqlen
                                     + e_ * 128,
                                     ap=[[PS_BF, 128], [seqlen, 2], [1, 128]])
                            sc.activation(out=dst, in_=srcp, func=AF.Copy
                                          ).then_inc(SM[sTEVSn], 1)

            ln_sc(NTF, A_XF, A_SQ1, A_ST,
                  (("sXF0", "sXF1"), "sSQ", "sVAR", "sSD", "sTR",
                   "sTEVS", O_XNTF, S))
            sc.wait_ge(SM["sTEVS"], CNT_TR[1][8 * NTF])
            bar(sc, BAR_B)
            if upto < 2:
                return

            sc.wait_ge(SM["sKST"], 16 * 128)
            for vc in range(4):
                for tt in range(NTF):
                    vv = 32 * vc + tt
                    sc.wait_ge(SM["sVMM"], vv + 1)
                    if vv >= 3:
                        sc.wait_ge(SM["sVST"], 16 * (vv - 2))
                    sc.activation(out=vb(B_VO + (vv % 3) * 1024, 512),
                                  in_=psap(vv % 2, 512),
                                  func=AF.Copy).then_inc(SM["sVEV"], 1)
            sc.wait_ge(SM["sVEV"], 128)
            bar(sc, BAR_C)
            if upto < 3:
                return

            ln_sc(NT, C_XF, C_SQ, C_ST,
                  (("sXF2a", "sXF2b"), "sSQ2", "sVAR2", "sSD2",
                   "sTR2", "sTEVS2", O_XNTM, TOK))
            sc.wait_ge(SM["sTEVS2"], CNT_TR2[1][8 * NT])
            bar(sc, BAR_D)
            if upto < 4:
                return
            bar(sc, BAR_E)
            if upto < 5:
                return

            def sc_stageA(L):
                h, lt = L // NT, L % NT
                nk = NTK[lt] * 128
                sc.wait_ge(SM["sNM"], L + 1)
                if L >= 2:
                    sc.wait_ge(SM["sPT"], CUMT[L - 2] // 2)
                src = AP(tensor=arena, offset=(O_S2 + (L % 3) * 16384) // 4,
                         ap=[[PS_F32, 128], [1, nk]])
                dst = AP(tensor=arena_bf, offset=(O_P2 + (L % 2) * 8192) // 2,
                         ap=[[PS_BF, 128], [1, nk]])
                sc.activation(out=dst, in_=src, func=AF.Exp,
                              bias=vf(O_NM + (L % 3) * 16, 1)
                              ).then_inc(SM["sEXP"], 1)

            def sc_b_thunks(L):
                h, lt = L // NT, L % NT
                mp = {}
                first = [True]

                def mk(pr):
                    def f():
                        if first[0]:
                            first[0] = False
                            if L >= 2:
                                sc.wait_ge(SM["sPV"], L - 1)
                        gp = GP(h, lt, pr)
                        sc.wait_ge(SM["sPT"], gp + 1)
                        srcp = AP(tensor=PSQ[1 + gp % 2], offset=0,
                                  ap=[[1024, 128], [512, 2], [1, 128]])
                        dst = AP(tensor=arena_bf,
                                 offset=(O_PT2 + (L % 2) * 8192) // 2 + 2 * pr * 128,
                                 ap=[[PS_BF, 128], [1, 256]])
                        sc.activation(out=dst, in_=srcp, func=AF.Copy
                                      ).then_inc(SM["sPTEVS"], 1)
                    return f
                for pr in range(NTK[lt] // 2):
                    if GP(h, lt, pr) % 2 == 1:
                        mp[pr] = mk(pr)

                def oev():
                    sc.wait_ge(SM["sINV"], L + 1)
                    ctxd = AP(tensor=arena_bf,
                              offset=O_CTX // 2 + lt * 2048 + h * 128,
                              ap=[[PS_BF, 128], [1, 128]])
                    sc.activation(out=ctxd,
                                  in_=AP(tensor=PSQ[3], offset=512 * (lt % 2),
                                         ap=[[1024, 128], [1, 128]]),
                                  func=AF.Copy, scale=vf(O_INV + (L % 2) * 4, 1)
                                  ).then_inc(SM["sOEV"], 1)
                return mp, oev

            for L in range(NH * NT + 2):
                if L >= 2:
                    bmap, oev = sc_b_thunks(L - 2)
                else:
                    bmap, oev = {}, None
                for tok in p2_master(L):
                    if tok[0] == 'pt' and tok[1] in bmap:
                        bmap[tok[1]]()
                    elif tok[0] == 'oev':
                        oev()
                    elif tok[0] == 'exp':
                        sc_stageA(L)
            sc.wait_ge(SM["sOEV"], NH * NT)
            bar(sc, BAR_F)
            if upto < 6:
                return

            for lt in range(NT):
                for k in range(8):
                    q = 8 * lt + k
                    if q % 2 != 1:
                        continue
                    sc.wait_ge(SM["sOT"], q + 1)
                    srcp = AP(tensor=PSQ[1 + q % 2], offset=0,
                              ap=[[1024, 128], [512, 2], [1, 128]])
                    dst = AP(tensor=arena_bf,
                             offset=(O_CXT + (lt % 2) * 4096) // 2 + 2 * k * 128,
                             ap=[[PS_BF, 128], [1, 256]])
                    sc.activation(out=dst, in_=srcp, func=AF.Copy
                                  ).then_inc(SM["sOTEVS"], 1)
                for hc in range(4):
                    dd = 4 * lt + hc
                    sc.wait_ge(SM["sDMM"], dd + 1)
                    if dd >= 3:
                        sc.wait_ge(SM["sDST"], 16 * (dd - 2))
                    sc.activation(out=vf(O_SO3 + (dd % 3) * 2048, 512),
                                  in_=psap(dd % 2, 512),
                                  func=AF.Copy).then_inc(SM["sDEV"], 1)
            sc.wait_ge(SM["sDEV"], 64)
            bar(sc, BAR_G)
            if upto < 7:
                return

            sc.wait_ge(SM["sP4C"], 48)
            gs = vf(O_GS4, 44)
            for qr in range(4):
                for G_ in range(4):
                    for icl in range(11):
                        ic = 11 * G_ + icl
                        gi = 44 * qr + ic
                        sc.wait_ge(SM["sGU"], gi + 1)
                        if gi >= 2:
                            sc.wait_ge(SM["sHD4"], gi - 1)
                        sc.activation(out=vf(O_SG4 + (gi % 2) * 2048, 512),
                                      in_=psap(gi % 2, 512),
                                      func=AF.Silu, scale=gs[:, ic:ic + 1]
                                      ).then_inc(SM["sSG"], 1)
                for qt in range(4):
                    cb = 4 * qr + qt
                    sc.wait_ge(SM["sT1"], cb + 1)
                    sc.activation(out=vf(O_TMP2, H), in_=vf(O_TMP, H),
                                  func=AF.Silu).then_inc(SM["sT2"], 1)

    return nc


def _host_constants(parity):
    inv_freq = (1.0 / (10000.0 ** (np.arange(0, HD, 2, dtype=np.float32)
                                   / np.float32(HD)))).astype(np.float32)
    ang = (np.arange(S, dtype=np.float32)[:, None] * inv_freq[None, :]
           ).astype(np.float32)
    cos = np.cos(ang).astype(np.float32)
    sin = np.sin(ang).astype(np.float32)
    C = np.ascontiguousarray(np.concatenate([cos.T, cos.T], axis=0))
    Sg = np.ascontiguousarray(np.concatenate([-sin.T, sin.T], axis=0))

    scale = np.float32(1.0 / math.sqrt(HD))
    idx = ((np.arange(NT)[:, None] * 2 + parity) * 128
           + np.arange(128)[None, :]).reshape(-1)
    Cm = np.ascontiguousarray(C[:, idx] * scale).astype(np.float32)
    Sm = np.ascontiguousarray(Sg[:, idx] * scale).astype(np.float32)

    m = np.arange(4223)
    delta_true = (4095 - m) - 128 * (1 - parity)
    n = np.maximum(delta_true, 0).astype(np.int64)
    nf = np.maximum(n, 1).astype(np.float32)
    large = ((np.log(nf / np.float32(16)) / np.float32(math.log(8.0)))
             * np.float32(16)).astype(np.int32) + 16
    large = np.minimum(large, 31)
    bucket = np.where(n < 16, n, large).astype(np.int64)
    onehot = np.zeros((32, 4223), dtype=np.float32)
    onehot[bucket, np.arange(4223)] = 1.0
    maskv = np.where(delta_true < 0, np.float32(NEG), np.float32(0.0))
    maskv = np.ascontiguousarray(
        np.tile(maskv[None, :], (NH, 1)).astype(np.float32))
    return C, Sg, Cm, Sm, onehot, maskv


def make_in_maps(L0_attn_x, L0_attn_qkv_w, L0_attn_dense_w,
                 L0_attn_rel_attn_table, L0_ffn_gate_w, L0_ffn_up_w,
                 L0_ffn_down_w, L0_ffn_gate_s, L0_ffn_up_s, L0_ffn_down_s):
    x = np.ascontiguousarray(np.asarray(L0_attn_x, dtype=np.float32))
    qkv_w = np.ascontiguousarray(np.asarray(L0_attn_qkv_w, dtype=np.float32))
    dense_w = np.ascontiguousarray(np.asarray(L0_attn_dense_w, dtype=np.float32))
    rel = np.ascontiguousarray(np.asarray(L0_attn_rel_attn_table, dtype=np.float32))
    gate_w = np.asarray(L0_ffn_gate_w, dtype=np.float32)
    up_w = np.asarray(L0_ffn_up_w, dtype=np.float32)
    down_w = np.asarray(L0_ffn_down_w, dtype=np.float32)
    gate_s = np.asarray(L0_ffn_gate_s, dtype=np.float32)
    up_s = np.asarray(L0_ffn_up_s, dtype=np.float32)
    down_s = np.asarray(L0_ffn_down_s, dtype=np.float32)

    gateT = np.ascontiguousarray(gate_w.T)
    upT = np.ascontiguousarray(up_w.T)
    downT = np.ascontiguousarray(down_w.T)
    c31 = np.ascontiguousarray(rel[31, :].reshape(NH, 1))
    gate_sT = np.ascontiguousarray(gate_s.reshape(44, 128).T)
    up_sT = np.ascontiguousarray(up_s.reshape(44, 128).T)
    ds_bc = np.ascontiguousarray(np.tile(down_s[None, :], (128, 1)))
    ident = np.eye(128, dtype=ml_dtypes.bfloat16)

    consts = {p: _host_constants(p) for p in (0, 1)}
    in_maps = []
    for core in range(8):
        b, p = core // 2, core % 2
        C, Sg, Cm, Sm, onehot, maskv = consts[p]
        xb = x[b]
        x_my = np.ascontiguousarray(
            xb.reshape(NTF, 128, H)[p::2].reshape(TOK, H))
        in_maps.append({
            "x_full": xb, "x_my": x_my, "qkv_w": qkv_w, "dense_w": dense_w,
            "gateT": gateT, "upT": upT, "downT": downT, "rel": rel,
            "c31": c31, "onehot": onehot, "maskv": maskv,
            "ropeC": C, "ropeS": Sg, "ropeCm": Cm, "ropeSm": Sm,
            "ident": ident, "gate_sT": gate_sT, "up_sT": up_sT, "ds_bc": ds_bc,
        })
    return in_maps


def kernel(L0_attn_x, L0_attn_qkv_w, L0_attn_dense_w, L0_attn_rel_attn_table,
           L0_ffn_gate_w, L0_ffn_up_w, L0_ffn_down_w, L0_ffn_gate_s,
           L0_ffn_up_s, L0_ffn_down_s):
    in_maps = make_in_maps(L0_attn_x, L0_attn_qkv_w, L0_attn_dense_w,
                           L0_attn_rel_attn_table, L0_ffn_gate_w, L0_ffn_up_w,
                           L0_ffn_down_w, L0_ffn_gate_s, L0_ffn_up_s,
                           L0_ffn_down_s)
    nc = build_nc()
    res = run_bass_kernel_spmd(nc, in_maps, core_ids=list(range(8)))
    out = np.empty((B, S, H), dtype=np.float32)
    for core in range(8):
        b, p = core // 2, core % 2
        shard = res.results[core]["out"]
        out[b].reshape(NTF, 128, H)[p::2] = shard.reshape(NT, 128, H)
    return out


if __name__ == "__main__":
    nc = build_nc()
    print("instructions:", nc.n_instructions())
